# version 09
# baseline (speedup 1.0000x reference)
import numpy as np
from contextlib import ExitStack

import concourse.bass as bass
import concourse.mybir as mybir
from concourse.bass_utils import run_bass_kernel_spmd

F32 = mybir.dt.float32
BF16 = mybir.dt.bfloat16
U8 = mybir.dt.uint8
AF = mybir.ActivationFunctionType
ALU = mybir.AluOpType

DM = 2048
DI = 4096
DN = 16
DR = 128
DC = 4
NMLP = 16384
NCORE = 8
KCH = DM // 128
ICH = DI // 128
NCH = NMLP // 128


def rap(t, off, *dims):
    return bass.AP(t, int(off), [list(d) for d in dims])


class Sem:
    __slots__ = ("h", "n")

    def __init__(self, h):
        self.h = h
        self.n = 0


class Builder:
    def __init__(self, nc):
        self.nc = nc
        self.streams = {k: [] for k in ("sync", "vector", "scalar", "tensor", "gpsimd")}
        self.own = {}

    def emit(self, eng, waits=(), fn=None, inc=None, inck=1):
        waits = [(s, int(v)) for (s, v) in waits]
        own = self.own.get(eng)
        if own is not None and fn is not None and not (eng == "gpsimd" and inc is not None):
            waits.append((own, own.n))
            if inc is None:
                inc = own
                inck = 1
        if inc is not None:
            inc.n += inck

        def go(e):
            for (s, v) in waits:
                if v > 0:
                    e.wait_ge(s.h, v)
            ins = fn(e) if fn is not None else None
            if inc is not None:
                assert ins is not None, "inc requires an instruction"
                ins.then_inc(inc.h, inck)

        self.streams[eng].append(go)

    def run(self):
        nc = self.nc
        with nc.Block() as block:
            @block.sync
            def _(e):
                for f in self.streams["sync"]:
                    f(e)

            @block.vector
            def _(e):
                for f in self.streams["vector"]:
                    f(e)

            @block.scalar
            def _(e):
                for f in self.streams["scalar"]:
                    f(e)

            @block.tensor
            def _(e):
                for f in self.streams["tensor"]:
                    f(e)

            @block.gpsimd
            def _(e):
                for f in self.streams["gpsimd"]:
                    f(e)


def build_nc(T=2048, debug=False, stage=5):
    assert T % 8 == 0
    TM = T // 4
    TT = T // 4
    HALF = T // 2
    assert TT <= 512 and TM >= 8
    UN = ICH * DN

    nc = bass.Bass("TRN2", target_bir_lowering=False)

    def din(name, shape, dt=F32):
        return nc.dram_tensor(name, shape, dt, kind="ExternalInput")

    xin = din("xin", [DM, TM + T])
    w1t = din("w1t", [DM, 2 * DI])
    wxt = din("wxt", [DI, 176])
    wdt = din("wdt", [DR, DI])
    wot = din("wot", [DI, DM])
    convw = din("convw", [128, ICH * DC])
    dtb = din("dtb", [128, ICH])
    dvec = din("dvec", [128, ICH])
    alog = din("alog", [128, UN])
    maskd = din("maskd", [128, 8])
    maskdc = din("maskdc", [128, 8])
    maske = din("maske", [128, 8])
    idm = din("idm", [128, 128])
    onescol = din("onescol", [128, 1])
    kcst = din("kcst", [128, 8])
    gpk = din("gpk", [1024, 2048], U8)
    upk = din("upk", [1024, 2048], U8)
    dpk = din("dpk", [1024, 2048], U8)
    gsc = din("gsc", [64, 2048])
    usc = din("usc", [64, 2048])
    dsc = din("dsc", [64, 2048])

    out = nc.dram_tensor("out", [DM, T], F32, kind="ExternalOutput")

    dkind = "ExternalOutput" if debug else "Internal"

    def dint(name, shape, dt, dbg=False):
        return nc.dram_tensor(name, shape, dt, kind=(dkind if dbg else "Internal"))

    w1b = dint("w1b", [16, KCH, 128, 512], BF16)
    wob = dint("wob", [4, ICH, 128, 512], BF16)
    sz_sp = dint("sz_sp", [ICH, 128, T], BF16, dbg=True)
    xc_sp = dint("xc_sp", [ICH, 128, T], BF16, dbg=True)
    dl_sp = dint("dl_sp", [ICH, 128, T], BF16, dbg=True)
    du_sp = dint("du_sp", [ICH, 128, T], BF16, dbg=True)
    bmcm = dint("bmcm", [64, T], F32, dbg=True)
    muv = dint("muv", [2, 512], F32)
    e0loc = dint("e0loc", [128, 1024], F32)
    e0g = dint("e0g", [1024, 1024], F32)
    e0dbg = dint("e0dbg", [128, 1024], F32, dbg=True) if debug else None
    y2_sp = dint("y2_sp", [ICH, 128, T], BF16, dbg=True)
    x2_sp = dint("x2_sp", [KCH, 128, T], F32, dbg=True)
    wloc = dint("wloc", [6144, 2048], BF16)
    wldbg = dint("wldbg", [6144, 2048], BF16, dbg=True) if debug else None
    fdbg1 = dint("fdbg1", [128, 64 * 512], BF16, dbg=True) if debug else None
    fdbg2 = dint("fdbg2", [128, 16 * 512], F32, dbg=True) if debug else None
    fdbg3 = dint("fdbg3", [128, 16 * 512], BF16, dbg=True) if debug else None
    fdbg4 = dint("fdbg4", [128, 512], F32, dbg=True) if debug else None
    fdbg5 = dint("fdbg5", [128, 512], F32, dbg=True) if debug else None
    fdbg6 = dint("fdbg6", [128, 512], F32, dbg=True) if debug else None
    fdbg7 = dint("fdbg7", [128, 64 * 512], BF16, dbg=True) if debug else None
    wgth = dint("wgth", [8, 6144, 2048], BF16)

    ARENA = 206848
    es = ExitStack()
    es.__enter__()
    psum = [es.enter_context(nc.psum_tensor(f"pb{i}", [128, 512], F32)) for i in range(8)]
    backing = es.enter_context(nc.sbuf_tensor("arena", [128, ARENA], U8))
    base_addr = nc.lookup_mloc(backing).addr

    def sb(name, shape, dt, off):
        nbytes = int(np.prod(shape[1:])) * mybir.dt.size(dt)
        assert off + nbytes <= ARENA, (name, off, nbytes)
        return nc.alloc_sbuf_tensor_at(name, shape, dt, offset=base_addr + int(off))

    oR = [0]

    def resv(nbytes):
        off = oR[0]
        oR[0] = (oR[0] + int(nbytes) + 63) & ~63
        return off

    aneg = sb("aneg", [128, UN], F32, resv(UN * 4))
    apos = sb("apos", [128, UN], F32, resv(UN * 4))
    cw = sb("cw", [128, ICH * DC], F32, resv(ICH * DC * 4))
    dtbs = sb("dtbs", [128, ICH], F32, resv(ICH * 4))
    dvs = sb("dvs", [128, ICH], F32, resv(ICH * 4))
    mskd = sb("mskd", [128, 8], F32, resv(32))
    mskdc = sb("mskdc", [128, 8], F32, resv(32))
    mske = sb("mske", [128, 8], F32, resv(32))
    kcs = sb("kcs", [128, 8], F32, resv(32))
    idmb = sb("idmb", [128, 128], BF16, resv(256))
    onesb = sb("onesb", [128, 1], BF16, resv(64))
    tdp = sb("tdp", [128, ICH * 4], F32, resv(ICH * 4 * 4))
    td = sb("td", [128, ICH], F32, resv(ICH * 4))
    stat = sb("stat", [128, 1536], F32, resv(1536 * 4))
    e0a = sb("e0a", [128, UN], F32, resv(UN * 4))
    e0b = sb("e0b", [128, UN], F32, resv(UN * 4))
    dcal = sb("dcal", [128, UN], F32, resv(UN * 4))
    hin = sb("hin", [128, UN], F32, resv(UN * 4))
    hcar = sb("hcar", [128, UN], F32, resv(UN * 4))
    SCR = oR[0]

    def mk_scr():
        cur = [0]

        def f(nbytes):
            off = cur[0] + SCR
            cur[0] = (cur[0] + int(nbytes) + 63) & ~63
            return off
        return cur, f

    c, f = mk_scr()
    m_x = sb("m_x", [128, max(KCH * TM, 4096)], BF16, f(max(KCH * TM, 4096) * 2))
    m_xln = sb("m_xln", [128, KCH * TM], BF16, f(KCH * TM * 2))
    m_slab = [sb(f"m_slab{i}", [128, KCH * 512], BF16, f(KCH * 512 * 2)) for i in range(2)]
    m_xpre = sb("m_xpre", [128, ICH * (TM + 3)], BF16, f(ICH * (TM + 3) * 2))
    m_xc = sb("m_xc", [128, ICH * TM], BF16, f(ICH * TM * 2))
    wxtb = sb("wxtb", [128, ICH * 176], BF16, f(ICH * 176 * 2))
    wdtb = sb("wdtb", [128, DI], BF16, f(DI * 2))
    m_sz = [sb(f"m_sz{i}", [128, TM], BF16, f(TM * 2)) for i in range(2)]
    m_cv = [sb(f"m_cv{i}", [128, TM], F32, f(TM * 4)) for i in range(2)]
    m_dt = sb("m_dt", [128, TM], BF16, f(TM * 2))
    m_bc = sb("m_bc", [64, TM], F32, f(TM * 4))
    m_sp1 = sb("m_sp1", [128, TM], F32, f(TM * 4))
    m_sp2 = sb("m_sp2", [128, TM], F32, f(TM * 4))
    m_dl = [sb(f"m_dl{i}", [128, TM], BF16, f(TM * 2)) for i in range(2)]
    m_du = [sb(f"m_du{i}", [128, TM], BF16, f(TM * 2)) for i in range(2)]
    m_bc1 = sb("m_bc1", [128, 512], BF16, f(512 * 2))
    m_bc2 = sb("m_bc2", [128, 512], BF16, f(512 * 2))
    m_top = c[0]

    c, f = mk_scr()
    s_brep = sb("s_brep", [128, DN * HALF], BF16, f(DN * HALF * 2))
    s_crep = sb("s_crep", [128, DN * HALF], BF16, f(DN * HALF * 2))
    s_dl = [sb(f"s_dl{i}", [128, HALF], BF16, f(HALF * 2)) for i in range(2)]
    s_du = [sb(f"s_du{i}", [128, HALF], BF16, f(HALF * 2)) for i in range(2)]
    s_xc = [sb(f"s_xc{i}", [128, HALF], BF16, f(HALF * 2)) for i in range(2)]
    s_sz = [sb(f"s_sz{i}", [128, HALF], BF16, f(HALF * 2)) for i in range(2)]
    s_a = [sb(f"s_a{i}", [128, HALF], BF16, f(HALF * 2)) for i in range(2)]
    s_u = [sb(f"s_u{i}", [128, HALF], BF16, f(HALF * 2)) for i in range(2)]
    s_h = [sb(f"s_h{i}", [128, HALF], BF16, f(HALF * 2)) for i in range(2)]
    s_tmp = [sb(f"s_tmp{i}", [128, HALF], BF16, f(HALF * 2)) for i in range(2)]
    s_y1 = sb("s_y1", [128, TT], F32, f(TT * 4))
    s_y2 = sb("s_y2", [128, TT], BF16, f(TT * 2))
    s_e = [sb(f"s_e{i}", [128, 512], F32, f(512 * 4)) for i in range(2)]
    s_cd = sb("s_cd", [128, HALF], F32, f(HALF * 4))
    s_one = sb("s_one", [128, HALF], BF16, f(HALF * 2))
    s_bx = sb("s_bx", [128, 32], F32, f(128))
    p_a, p_u, p_h, p_t = s_a, s_u, s_h, s_tmp
    s_top = c[0]

    c, f = mk_scr()
    c[0] = max(s_top, 16384 + ((KCH * 512 * 2 + 63) & ~63))
    q_pk = [sb(f"q_pk{i}", [128, 2048], U8, f(2048)) for i in range(2)]
    q_u8 = sb("q_u8", [128, 2048], U8, f(2048))
    q_st = [sb(f"q_st{i}", [128, 2048], F32, f(2048 * 4)) for i in range(2)]
    q_vf = sb("q_vf", [128, 2048], F32, f(2048 * 4))
    q_b3 = sb("q_b3", [128, 2048], F32, f(2048 * 4))
    q_r1 = sb("q_r1", [128, 2048], F32, f(2048 * 4))
    q_r2 = sb("q_r2", [128, 2048], F32, f(2048 * 4))
    q_w = [sb(f"q_w{i}", [128, 2048], BF16, f(2048 * 2)) for i in range(2)]
    q_top = c[0]

    c, f = mk_scr()
    o_x2ln = sb("o_x2ln", [128, KCH * TT], BF16, f(KCH * TT * 2))
    f_base = c[0]
    o_y2 = sb("o_y2", [128, ICH * TT], BF16, f(ICH * TT * 2))
    o_xsq = sb("o_xsq", [128, KCH * TT], BF16, f(KCH * TT * 2))
    o_slab = [sb(f"o_slab{i}", [128, ICH * 512], BF16, f(ICH * 512 * 2)) for i in range(2)]
    o_x2 = sb("o_x2", [128, KCH * TT], F32, f(KCH * TT * 4))
    o_xp = [sb(f"o_xp{i}", [128, TT], F32, f(TT * 4)) for i in range(2)]
    o_bc1 = sb("o_bc1", [128, 512], BF16, f(512 * 2))
    o_bc2 = sb("o_bc2", [128, 512], BF16, f(512 * 2))
    o_top = c[0]

    c, f = mk_scr()
    c[0] = f_base
    f_int = sb("f_int", [128, (NCH // 2) * TT], BF16, f((NCH // 2) * TT * 2))
    f_acc = sb("f_acc", [128, KCH * TT], F32, f(KCH * TT * 4))
    f_g = [sb(f"f_g{i}", [128, 16 * 256], BF16, f(16 * 256 * 2)) for i in range(2)]
    f_u = [sb(f"f_u{i}", [128, 16 * 256], BF16, f(16 * 256 * 2)) for i in range(2)]
    f_d = [sb(f"f_d{i}", [128, 16 * 256], BF16, f(16 * 256 * 2)) for i in range(2)]
    f_gg = [sb(f"f_gg{i}", [128, TT], F32, f(TT * 4)) for i in range(2)]
    f_x2p = [sb(f"f_x2p{i}", [128, TT], F32, f(TT * 4)) for i in range(2)]
    f_o = [sb(f"f_o{i}", [128, TT], F32, f(TT * 4)) for i in range(2)]
    f_top = c[0]

    c, f = mk_scr()
    p_w = [sb(f"p_w{i}", [128, 8192], F32, f(8192 * 4)) for i in range(2)]
    p_wb = [sb(f"p_wb{i}", [128, 8192], BF16, f(8192 * 2)) for i in range(2)]
    p_x1 = sb("p_x1", [128, ICH * 176], F32, f(ICH * 176 * 4))
    p_x2 = sb("p_x2", [128, DI], F32, f(DI * 4))
    p_top = c[0]

    total = SCR + max(m_top, s_top, q_top, o_top, f_top, p_top)
    assert total <= ARENA, f"SBUF overflow: {total}"

    _names = ["pe", "ve", "sc", "cc",
              "qc", "qp0", "qp1", "qx", "qsl0", "qsl1", "qy2",
              "qfg0", "qfg1", "qfu0", "qfu1", "qfd0", "qfd1",
              "qz0", "qz1", "qdl0", "qdl1", "qdu0", "qdu1",
              "qxc", "qbm", "qmu", "qbr", "qld0", "qld1", "qe",
              "qw80", "qw81", "qdq", "qy", "qx2", "qx2p0", "qx2p1", "qo0", "qo1",
              "qcs0", "qcs1", "qst", "qxp0", "qxp1", "gp"]
    sems = {k: Sem(es.enter_context(nc.semaphore(k))) for k in _names}
    pe, ve, scm, cc = (sems[k] for k in ("pe", "ve", "sc", "cc"))
    gpo = sems["gp"]
    qc, qx, qy2 = sems["qc"], sems["qx"], sems["qy2"]
    qp = [sems["qp0"], sems["qp1"]]
    qsl = [sems["qsl0"], sems["qsl1"]]
    qfg = [sems["qfg0"], sems["qfg1"]]
    qfu = [sems["qfu0"], sems["qfu1"]]
    qfd = [sems["qfd0"], sems["qfd1"]]
    qz = [sems["qz0"], sems["qz1"]]
    qdl = [sems["qdl0"], sems["qdl1"]]
    qdu = [sems["qdu0"], sems["qdu1"]]
    qxc, qbm, qmu, qbr, qe = sems["qxc"], sems["qbm"], sems["qmu"], sems["qbr"], sems["qe"]
    qld = [sems["qld0"], sems["qld1"]]
    qw8 = [sems["qw80"], sems["qw81"]]
    qdq, qy, qx2 = sems["qdq"], sems["qy"], sems["qx2"]
    qx2p = [sems["qx2p0"], sems["qx2p1"]]
    qo = [sems["qo0"], sems["qo1"]]
    qcs = [sems["qcs0"], sems["qcs1"]]
    qst = sems["qst"]
    qxp = [sems["qxp0"], sems["qxp1"]]

    B = Builder(nc)
    B.own = {"vector": sems["ve"], "scalar": sems["sc"], "gpsimd": sems["gp"]}
    em = B.emit
    guard = {}

    def gw(*keys):
        w = []
        for k in keys:
            w += guard.get(k, [])
        return w

    def gs(key, *ss):
        guard[key] = [(s, s.n) for s in ss]

    for (dsts, srct) in ((cw, convw), (dtbs, dtb), (dvs, dvec), (mskd, maskd),
                         (mskdc, maskdc), (mske, maske), (kcs, kcst)):
        em("sync", (), (lambda dsts, srct: lambda e: e.dma_start(out=dsts[:], in_=srct.ap()))(dsts, srct), qc, 16)
    mxf = m_x[:].bitcast(F32)
    em("sync", (), lambda e: e.dma_start(out=mxf[:, 0:128], in_=idm.ap()), qc, 16)
    em("sync", (), lambda e: e.dma_start(out=mxf[:, 128:129], in_=onescol.ap()), qc, 16)
    em("sync", (), lambda e: e.dma_start(out=mxf[:, 129:129 + UN], in_=alog.ap()), qc, 16)
    em("sync", (), lambda e: e.dma_start(
        out=p_x1[:].rearrange("p (a b) -> p a b", a=ICH),
        in_=wxt.ap().rearrange("(a p) b -> p a b", p=128)), qc, 16)
    em("sync", (), lambda e: e.dma_start(out=p_x2[:], in_=wdt.ap()), qc, 16)
    n_qc = qc.n
    em("vector", [(qc, n_qc)], lambda e: e.tensor_copy(idmb[:], mxf[:, 0:128]), ve, 1)
    em("vector", (), lambda e: e.tensor_copy(onesb[:], mxf[:, 128:129]), ve, 1)
    em("scalar", [(qc, n_qc)], lambda e: e.activation(apos[:], mxf[:, 129:129 + UN], AF.Exp), scm, 1)
    em("vector", [(scm, scm.n)], lambda e: e.tensor_scalar_mul(aneg[:], apos[:], -1.0), ve, 1)
    em("vector", (), lambda e: e.memset(tdp[:], 0.0), ve, 1)

    cast_guard = {0: ve.n, 1: ve.n}
    cast_sc0 = scm.n

    def cast_weight(src, dst, kchunks, mlen):
        msup = mlen // 512
        for kc in range(kchunks):
            b = kc % 2
            em("sync", [(ve, cast_guard[b]), (scm, cast_sc0)],
               (lambda kc, b, mlen, src: lambda e: e.dma_start(
                   out=p_w[b][:, 0:mlen],
                   in_=src.ap()[kc * 128:(kc + 1) * 128, :]))(kc, b, mlen, src), qp[b], 16)
            em("vector", [(qp[b], qp[b].n), (qcs[b], qcs[b].n)],
               (lambda b, mlen: lambda e: e.tensor_copy(
                   p_wb[b][:, 0:mlen], p_w[b][:, 0:mlen]))(b, mlen), ve, 1)
            cast_guard[b] = ve.n
            em("gpsimd", [(ve, ve.n)], (lambda kc, b, msup, kchunks, mlen, dst: lambda e: e.dma_start(
                out=rap(dst, kc * 128 * 512, (512, 128), (kchunks * 128 * 512, msup), (1, 512)),
                in_=p_wb[b][:, 0:mlen].rearrange("p (s m) -> p s m", m=512)))(kc, b, msup, kchunks, mlen, dst),
               qcs[b], 16)

    cast_weight(w1t, w1b, KCH, 2 * DI)
    cast_weight(wot, wob, ICH, DM)
    n_cs = (qcs[0].n, qcs[1].n)
    em("vector", [(qc, n_qc), (qcs[0], n_cs[0]), (qcs[1], n_cs[1])],
       lambda e: e.tensor_copy(wxtb[:], p_x1[:]), ve, 1)
    em("vector", (), lambda e: e.tensor_copy(wdtb[:], p_x2[:]), ve, 1)
    for _b in range(2):
        guard[("msl", _b)] = [(qcs[0], n_cs[0]), (qcs[1], n_cs[1]), (ve, ve.n)]
        guard[("osl", _b)] = [(qcs[0], n_cs[0]), (qcs[1], n_cs[1]), (ve, ve.n)]

    def ln_stats_apply(xbf, xsq, xlnout, ncols, in_waits, bc1, bc2):
        em("scalar", list(in_waits), lambda e: e.activation(
            xsq[:, 0:KCH * ncols], xbf[:, 0:KCH * ncols], AF.Square), scm, 1)
        vready, sready = ve.n, scm.n

        def mm1(e):
            for kc in range(KCH):
                ins = nc.tensor.matmul(psum[4][0:1, 0:ncols], onesb[:, 0:1],
                                       xbf[:, kc * ncols:(kc + 1) * ncols],
                                       start=(kc == 0), stop=(kc == KCH - 1))
            return ins

        def mm2(e):
            for kc in range(KCH):
                ins = nc.tensor.matmul(psum[5][0:1, 0:ncols], onesb[:, 0:1],
                                       xsq[:, kc * ncols:(kc + 1) * ncols],
                                       start=(kc == 0), stop=(kc == KCH - 1))
            return ins
        em("tensor", list(in_waits) + gw("pb4"), mm1, pe, 1)
        em("tensor", [(scm, sready)] + gw("pb5"), mm2, pe, 1)
        pdone = pe.n
        em("vector", [(pe, pdone)], lambda e: e.tensor_scalar_mul(stat[0:1, 0:ncols], psum[4][0:1, 0:ncols], 1.0 / DM), ve, 1)
        gs("pb4", ve)
        em("vector", (), lambda e: e.tensor_scalar_mul(stat[0:1, 512:512 + ncols], psum[5][0:1, 0:ncols], 1.0 / DM), ve, 1)
        gs("pb5", ve)
        em("vector", (), lambda e: e.tensor_mul(stat[0:1, 1024:1024 + ncols], stat[0:1, 0:ncols], stat[0:1, 0:ncols]), ve, 1)
        em("vector", (), lambda e: e.tensor_sub(stat[0:1, 512:512 + ncols], stat[0:1, 512:512 + ncols], stat[0:1, 1024:1024 + ncols]), ve, 1)
        em("scalar", [(ve, ve.n)], lambda e: e.activation(stat[0:1, 1024:1024 + ncols], stat[0:1, 512:512 + ncols], AF.Ln, bias=kcs[0:1, 0:1]), scm, 1)
        em("scalar", (), lambda e: e.activation(stat[0:1, 512:512 + ncols], stat[0:1, 1024:1024 + ncols], AF.Exp, scale=kcs[0:1, 1:2]), scm, 1)
        em("vector", [(scm, scm.n)], lambda e: e.tensor_mul(stat[0:1, 1024:1024 + ncols], stat[0:1, 0:ncols], stat[0:1, 512:512 + ncols]), ve, 1)
        vfin = ve.n
        em("gpsimd", [(ve, vfin), (qmu, qmu.n)], lambda e: e.dma_start(out=rap(muv, 0, (512, 1), (1, ncols)), in_=stat[0:1, 512:512 + ncols]), qmu, 16)
        em("gpsimd", (), lambda e: e.dma_start(out=rap(muv, 512, (512, 1), (1, ncols)), in_=stat[0:1, 1024:1024 + ncols]), qmu, 16)
        gn = qmu.n
        em("gpsimd", [(qmu, gn)], (lambda bc1: lambda e: e.dma_start(out=bc1[:, 0:ncols], in_=rap(muv, 0, (0, 128), (1, ncols))))(bc1), qmu, 16)
        em("gpsimd", (), (lambda bc2: lambda e: e.dma_start(out=bc2[:, 0:ncols], in_=rap(muv, 512, (0, 128), (1, ncols))))(bc2), qmu, 16)
        gn2 = qmu.n
        for kc in range(KCH):
            em("vector", [(qmu, gn2)], (lambda kc: lambda e: e.tensor_mul(
                xlnout[:, kc * ncols:(kc + 1) * ncols],
                xbf[:, kc * ncols:(kc + 1) * ncols], bc1[:, 0:ncols]))(kc), ve, 1)
            em("vector", (), (lambda kc: lambda e: e.tensor_sub(
                xlnout[:, kc * ncols:(kc + 1) * ncols],
                xlnout[:, kc * ncols:(kc + 1) * ncols], bc2[:, 0:ncols]))(kc), ve, 1)
        gs("obc", ve)

    for j in range(5):
        is_halo = (j == 0)
        xw = [(ve, ve.n), (scm, scm.n), (qx, qx.n)]
        if j == 0:
            xw += [(qcs[0], n_cs[0]), (qcs[1], n_cs[1])]
        em("gpsimd", xw, (lambda j: lambda e: e.dma_start(
            out=m_x[:, 0:KCH * TM].rearrange("p (c t) -> p c t", c=KCH),
            in_=xin.ap().rearrange("(c p) t -> p c t", p=128)[:, :, j * TM:(j + 1) * TM]))(j), qx, 16)
        ln_stats_apply(m_x, m_xln, m_xln, TM, [(qx, qx.n)], m_bc1, m_bc2)
        xln_v = ve.n

        mlim = 16 if not is_halo else 8
        for ms in range(mlim):
            sb_i = ms % 2
            em("sync", gw(("msl", sb_i)), (lambda ms, sb_i: lambda e: e.dma_start(
                out=m_slab[sb_i][:].rearrange("p (a b) -> p a b", b=512),
                in_=rap(w1b, ms * KCH * 128 * 512, (512, 128), (128 * 512, KCH), (1, 512))))(ms, sb_i),
               qsl[sb_i], 16)
            slab_w = (qsl[sb_i], qsl[sb_i].n)
            for mc in range(4):
                m = ms * 4 + mc
                bank = mc % 2

                def mmgrp(mc, bank, sb_i):
                    def go(e):
                        for kc in range(KCH):
                            ins = nc.tensor.matmul(
                                psum[bank][:, 0:TM],
                                m_slab[sb_i][:, kc * 512 + mc * 128:kc * 512 + mc * 128 + 128],
                                m_xln[:, kc * TM:(kc + 1) * TM],
                                start=(kc == 0), stop=(kc == KCH - 1))
                        return ins
                    return go
                em("tensor", [slab_w, (ve, xln_v)] + gw(("pb", bank)), mmgrp(mc, bank, sb_i), pe, 1)
                mm_n = pe.n
                if mc == 3:
                    gs(("msl", sb_i), pe)
                if m < ICH:
                    em("vector", [(pe, mm_n)], (lambda m, bank: lambda e: e.tensor_copy(
                        m_xpre[:, m * (TM + 3) + 3:m * (TM + 3) + 3 + TM], psum[bank][:, 0:TM]))(m, bank), ve, 1)
                    gs(("pb", bank), ve)
                else:
                    zch = m - ICH
                    bq = zch % 2
                    em("scalar", [(pe, mm_n)] + gw(("msz", bq)), (lambda bank, bq: lambda e: e.activation(
                        m_sz[bq][:], psum[bank][:, 0:TM], AF.Silu))(bank, bq), scm, 1)
                    gs(("pb", bank), scm)
                    em("gpsimd", [(scm, scm.n)], (lambda zch, j, bq: lambda e: e.dma_start(
                        out=rap(sz_sp, zch * 128 * T + (j - 1) * TM, (T, 128), (1, TM)),
                        in_=m_sz[bq][:]))(zch, j, bq), qz[bq], 16)
                    gs(("msz", bq), sems["qz%d" % bq])
        evac_v = ve.n

        if is_halo:
            def tailcopy0(e):
                for chx in range(ICH):
                    ins = e.tensor_copy(m_xpre[:, chx * (TM + 3):chx * (TM + 3) + 3],
                                        m_xpre[:, chx * (TM + 3) + TM:chx * (TM + 3) + TM + 3])
                return ins
            em("vector", [(ve, evac_v)], tailcopy0, ve, 1)
            continue

        for chx in range(ICH):
            b = chx % 2
            base = chx * (TM + 3)

            em("vector", [(scm, scm.n)], (lambda chx, b, base: lambda e: e.tensor_scalar(
                m_cv[b][:], m_xpre[:, base:base + TM],
                cw[:, chx * DC:chx * DC + 1], None, ALU.mult))(chx, b, base), ve, 1)
            for w in range(1, DC):
                em("vector", (), (lambda chx, b, base, w: lambda e: e.scalar_tensor_tensor(
                    m_cv[b][:], m_xpre[:, base + w:base + w + TM],
                    cw[:, chx * DC + w:chx * DC + w + 1], m_cv[b][:],
                    ALU.mult, ALU.add))(chx, b, base, w), ve, 1)
            em("scalar", [(ve, ve.n)] + (gw("mxc") if chx == 0 else []), (lambda chx, b: lambda e: e.activation(
                m_xc[:, chx * TM:(chx + 1) * TM], m_cv[b][:], AF.Silu))(chx, b), scm, 1)
        conv_s = scm.n

        def tailcopy(e):
            for chx in range(ICH):
                ins = e.tensor_copy(m_xpre[:, chx * (TM + 3):chx * (TM + 3) + 3],
                                    m_xpre[:, chx * (TM + 3) + TM:chx * (TM + 3) + TM + 3])
            return ins
        em("vector", [(scm, conv_s)], tailcopy, ve, 1)
        em("gpsimd", [(scm, conv_s)], (lambda j: lambda e: e.dma_start(
            out=rap(xc_sp, (j - 1) * TM, (T, 128), (128 * T, ICH), (1, TM)),
            in_=m_xc[:].rearrange("p (c t) -> p c t", c=ICH)))(j), qxc, 16)
        gs("mxc", qxc)

        def xp_dt(e):
            for kc in range(ICH):
                ins = nc.tensor.matmul(psum[6][:, 0:TM], wxtb[:, kc * 176:kc * 176 + 128],
                                       m_xc[:, kc * TM:(kc + 1) * TM],
                                       start=(kc == 0), stop=(kc == ICH - 1))
            return ins

        def xp_bc(e):
            for kc in range(ICH):
                ins = nc.tensor.matmul(psum[7][0:48, 0:TM], wxtb[:, kc * 176 + 128:kc * 176 + 176],
                                       m_xc[:, kc * TM:(kc + 1) * TM],
                                       start=(kc == 0), stop=(kc == ICH - 1))
            return ins
        em("tensor", [(scm, conv_s)] + gw("pb6"), xp_dt, pe, 1)
        em("tensor", gw("pb7"), xp_bc, pe, 1)
        xp_n = pe.n
        em("vector", [(pe, xp_n)], lambda e: e.tensor_copy(m_dt[:], psum[6][:, 0:TM]), ve, 1)
        gs("pb6", ve)
        em("vector", [(pe, xp_n), (qbm, qbm.n)], lambda e: e.tensor_copy(m_bc[0:48, 0:TM], psum[7][0:48, 0:TM]), ve, 1)
        gs("pb7", ve)
        em("gpsimd", [(ve, ve.n)], (lambda j: lambda e: e.dma_start(
            out=rap(bmcm, (j - 1) * TM, (T, 48), (1, TM)), in_=m_bc[0:48, 0:TM]))(j), qbm, 16)

        dt_v = ve.n
        for chx in range(ICH):
            bank = chx % 2
            bq = chx % 2
            em("tensor", [(ve, dt_v)] + gw(("pb", bank)), (lambda chx, bank: lambda e: nc.tensor.matmul(
                psum[bank][:, 0:TM], wdtb[:, chx * 128:(chx + 1) * 128], m_dt[:],
                start=True, stop=True))(chx, bank), pe, 1)
            mm_n = pe.n
            em("scalar", [(pe, mm_n)] + gw("msp"), (lambda chx, bank: lambda e: e.activation(
                m_sp1[:], psum[bank][:, 0:TM], AF.Relu, bias=dtbs[:, chx:chx + 1]))(chx, bank), scm, 1)
            em("scalar", (), (lambda chx, bank: lambda e: e.activation(
                m_sp2[:], psum[bank][:, 0:TM], AF.Abs, bias=dtbs[:, chx:chx + 1]))(chx, bank), scm, 1)
            gs(("pb", bank), scm)
            em("scalar", (), lambda e: e.activation(m_sp2[:], m_sp2[:], AF.Exp, scale=kcs[:, 2:3]), scm, 1)
            em("scalar", (), lambda e: e.activation(m_sp2[:], m_sp2[:], AF.Ln, bias=kcs[:, 3:4]), scm, 1)
            sp_s = scm.n
            em("vector", [(scm, sp_s), (qdl[bq], qdl[bq].n)], (lambda chx, bq, j: lambda e: e.scalar_tensor_tensor(
                m_dl[bq][:], m_sp1[:], 1.0, m_sp2[:], ALU.mult, ALU.add,
                accum_out=tdp[:, chx * 4 + (j - 1):chx * 4 + j]))(chx, bq, j), ve, 1)
            gs("msp", ve)
            em("vector", [(qdu[bq], qdu[bq].n)], (lambda chx, bq: lambda e: e.tensor_mul(
                m_du[bq][:], m_dl[bq][:], m_xc[:, chx * TM:(chx + 1) * TM]))(chx, bq), ve, 1)
            dl_v = ve.n
            em("gpsimd", [(ve, dl_v)], (lambda chx, bq, j: lambda e: e.dma_start(
                out=rap(dl_sp, chx * 128 * T + (j - 1) * TM, (T, 128), (1, TM)),
                in_=m_dl[bq][:]))(chx, bq, j), qdl[bq], 16)
            em("gpsimd", (), (lambda chx, bq, j: lambda e: e.dma_start(
                out=rap(du_sp, chx * 128 * T + (j - 1) * TM, (T, 128), (1, TM)),
                in_=m_du[bq][:]))(chx, bq, j), qdu[bq], 16)

    m_v = ve.n
    spill_done = [(qz[0], qz[0].n), (qz[1], qz[1].n), (qdl[0], qdl[0].n), (qdl[1], qdl[1].n),
                  (qdu[0], qdu[0].n), (qdu[1], qdu[1].n), (qxc, qxc.n), (qbm, qbm.n)]

    em("vector", [(ve, m_v)], lambda e: e.tensor_add(
        td[:], tdp[:].rearrange("p (c f) -> p c f", f=4)[:, :, 0],
        tdp[:].rearrange("p (c f) -> p c f", f=4)[:, :, 1]), ve, 1)
    em("vector", (), lambda e: e.tensor_add(
        stat[:, 0:ICH], tdp[:].rearrange("p (c f) -> p c f", f=4)[:, :, 2],
        tdp[:].rearrange("p (c f) -> p c f", f=4)[:, :, 3]), ve, 1)
    em("vector", (), lambda e: e.tensor_add(td[:], td[:], stat[:, 0:ICH]), ve, 1)
    td_v = ve.n

    def dc_ops(e):
        for chx in range(ICH):
            ins = e.activation(dcal[:, chx * DN:(chx + 1) * DN], aneg[:, chx * DN:(chx + 1) * DN],
                               AF.Exp, scale=td[:, chx:chx + 1])
        return ins
    em("scalar", [(ve, td_v)], dc_ops, scm, 1)
    dc_s = scm.n

    first_pass_load = [True]

    def build_reps(half, with_c, bw=()):
        for n in range(DN):
            em("gpsimd", list(bw) if n == 0 else (), (lambda n, half: lambda e: e.dma_start(
                out=s_brep[:, n * HALF:(n + 1) * HALF],
                in_=rap(bmcm, n * T + half * HALF, (0, 128), (1, HALF))))(n, half), qbr, 16)
            if with_c:
                em("gpsimd", (), (lambda n, half: lambda e: e.dma_start(
                    out=s_crep[:, n * HALF:(n + 1) * HALF],
                    in_=rap(bmcm, (32 + n) * T + half * HALF, (0, 128), (1, HALF))))(n, half), qbr, 16)
        return qbr.n

    POOL_N = ()

    def scan_pass(pass2, pre_reps=None, hook=None):
        for half in range(2):
            bw = [(ve, ve.n), (scm, scm.n), (gpo, gpo.n)]
            sync_gate = []
            if first_pass_load[0]:
                bw = bw + spill_done
                sync_gate = bw
                first_pass_load[0] = False
            if pass2 and half == 0 and pre_reps is not None:
                reps_g = pre_reps
            else:
                reps_g = build_reps(half, with_c=pass2, bw=bw)
            for chx in range(ICH):
                b = chx % 2
                em("sync", gw(("sld", b)) + (sync_gate if chx < 2 else []), (lambda chx, b, half: lambda e: e.dma_start(
                    out=s_dl[b][:], in_=rap(dl_sp, chx * 128 * T + half * HALF, (T, 128), (1, HALF))))(chx, b, half), qld[b], 16)
                em("sync", (), (lambda chx, b, half: lambda e: e.dma_start(
                    out=s_du[b][:], in_=rap(du_sp, chx * 128 * T + half * HALF, (T, 128), (1, HALF))))(chx, b, half), qld[b], 16)
                if pass2:
                    em("sync", (), (lambda chx, b, half: lambda e: e.dma_start(
                        out=s_xc[b][:], in_=rap(xc_sp, chx * 128 * T + half * HALF, (T, 128), (1, HALF))))(chx, b, half), qld[b], 16)
                    em("sync", (), (lambda chx, b, half: lambda e: e.dma_start(
                        out=s_sz[b][:], in_=rap(sz_sp, chx * 128 * T + half * HALF, (T, 128), (1, HALF))))(chx, b, half), qld[b], 16)
                ld_g = (qld[b], qld[b].n)
                if not pass2:
                    em("vector", [ld_g, (scm, scm.n)], (lambda b: lambda e: e.tensor_tensor_scan(
                        s_cd[:], s_one[:], s_dl[b][:], 0.0, ALU.mult, ALU.add))(b), ve, 1)
                    em("vector", (), (lambda chx: lambda e: e.tensor_scalar(
                        s_bx[:, 0:DN], aneg[:, chx * DN:(chx + 1) * DN],
                        s_cd[:, HALF - 1:HALF], None, ALU.mult))(chx), ve, 1)
                    cd_v = ve.n
                    em("scalar", [(ve, cd_v)], lambda e: e.activation(
                        s_bx[:, 16:16 + DN], s_bx[:, 0:DN], AF.Exp), scm, 1)
                    dch_s = scm.n
                for n in range(DN):
                    unit = chx * DN + n
                    ab = n % 2
                    pool = n in POOL_N
                    ceng = "gpsimd" if pool else "vector"
                    csem = gpo if pool else ve
                    t_u = (p_u if pool else s_u)[ab]
                    t_t = (p_t if pool else s_tmp)[ab]
                    if not pass2:
                        em("scalar", [(ve, cd_v)] + gw(("sa", ab)), (lambda chx, n, ab: lambda e: e.activation(
                            s_a[ab][:], s_cd[:], AF.Exp,
                            scale=apos[:, chx * DN + n:chx * DN + n + 1],
                            bias=s_bx[:, n:n + 1]))(chx, n, ab), scm, 1)
                        a_s = scm.n
                        em("vector", [(qbr, reps_g), ld_g], (lambda n, b, ab: lambda e: e.tensor_mul(
                            s_u[ab][:], s_du[b][:], s_brep[:, n * HALF:(n + 1) * HALF]))(n, b, ab), ve, 1)
                        em("vector", [(scm, a_s)], (lambda ab, unit, half: lambda e: e.scalar_tensor_tensor(
                            s_tmp[ab][:], s_a[ab][:], 1.0, s_u[ab][:], ALU.mult, ALU.mult,
                            accum_out=(e0a if half == 0 else e0b)[:, unit:unit + 1]))(ab, unit, half), ve, 1)
                        gs(("sa", ab), ve)
                        continue
                    em("scalar", [ld_g] + gw(("sa", ab)), (lambda chx, n, b, ab: lambda e: e.activation(
                        s_a[ab][:], s_dl[b][:], AF.Exp,
                        scale=aneg[:, chx * DN + n:chx * DN + n + 1]))(chx, n, b, ab), scm, 1)
                    a_s = scm.n
                    em(ceng, [(qbr, reps_g), ld_g] + gw(("su", pool, ab)), (lambda n, b, t_u: lambda e: e.tensor_mul(
                        t_u[:], s_du[b][:], s_brep[:, n * HALF:(n + 1) * HALF]))(n, b, t_u),
                       None if pool else ve, 1)
                    u_c = csem.n
                    uw = [(gpo, u_c)] if pool else []
                    if True:
                        em("vector", [(scm, a_s)] + uw + gw(("sh", ab)), (lambda ab, t_u, unit, half: lambda e: e.tensor_tensor_scan(
                            s_h[ab][:], s_a[ab][:], t_u[:],
                            (hin[:, unit:unit + 1] if half == 0 else hcar[:, unit:unit + 1]),
                            ALU.mult, ALU.add))(ab, t_u, unit, half), ve, 1)
                        gs(("sa", ab), ve)
                        gs(("su", pool, ab), ve)
                        scan_v = ve.n
                        if half == 0:
                            em("scalar", [(ve, scan_v)], (lambda ab, unit: lambda e: e.copy(
                                hcar[:, unit:unit + 1], s_h[ab][:, HALF - 1:HALF]))(ab, unit), scm, 1)
                            gs(("sh", ab), scm)
                        em(ceng, ([(ve, scan_v)] if pool else []) + gw(("stmp", pool, ab)),
                           (lambda n, ab, t_t: lambda e: e.tensor_mul(
                               t_t[:], s_h[ab][:], s_crep[:, n * HALF:(n + 1) * HALF]))(n, ab, t_t),
                           None if pool else ve, 1)
                        tmp_c = csem.n

                        def ymm(n, t_t):
                            def go(e):
                                for s_i in range(2):
                                    ins = nc.tensor.matmul(psum[s_i][:, 0:TT], idmb[:],
                                                           t_t[:, s_i * TT:(s_i + 1) * TT],
                                                           start=(n == 0), stop=(n == DN - 1))
                                return ins
                            return go
                        em("tensor", [(csem, tmp_c)] + (gw("pby") if n == 0 else []), ymm(n, t_t), pe, 1)
                        gs(("stmp", pool, ab), pe)
                if not pass2:
                    if half == 1:
                        em("vector", [(scm, dch_s)], (lambda chx: lambda e: e.tensor_mul(
                            s_e[0][:, 0:DN], e0a[:, chx * DN:(chx + 1) * DN],
                            s_bx[:, 16:16 + DN]))(chx), ve, 1)
                        em("vector", (), (lambda chx: lambda e: e.tensor_add(
                            e0b[:, chx * DN:(chx + 1) * DN], e0b[:, chx * DN:(chx + 1) * DN],
                            s_e[0][:, 0:DN]))(chx), ve, 1)
                    gs(("sld", b), scm, ve, gpo)
                if pass2:
                    y_p = pe.n
                    for s_i in range(2):
                        em("vector", [(pe, y_p), (qy, qy.n)], (lambda chx, b, s_i: lambda e: e.scalar_tensor_tensor(
                            s_y1[:], s_xc[b][:, s_i * TT:(s_i + 1) * TT], dvs[:, chx:chx + 1],
                            psum[s_i][:, 0:TT], ALU.mult, ALU.add))(chx, b, s_i), ve, 1)
                        em("vector", (), (lambda b, s_i: lambda e: e.tensor_mul(
                            s_y2[:], s_y1[:], s_sz[b][:, s_i * TT:(s_i + 1) * TT]))(b, s_i), ve, 1)
                        em("sync", [(ve, ve.n)], (lambda chx, half, s_i: lambda e: e.dma_start(
                            out=rap(y2_sp, chx * 128 * T + half * HALF + s_i * TT, (T, 128), (1, TT)),
                            in_=s_y2[:]))(chx, half, s_i), qy, 16)
                    gs("pby", ve)
                    gs(("sld", b), scm, ve, gpo)
                if hook is not None:
                    hook()

    def finish():
        if debug and stage >= 2:
            em("gpsimd", [(cc, cc.n), (ve, ve.n), (scm, scm.n), (pe, pe.n)],
               lambda e: e.dma_start(out=e0dbg.ap(), in_=e0loc.ap()), sems["qe"], 16)
            if stage >= 3:
                em("gpsimd", (), lambda e: e.dma_start(out=wldbg.ap(), in_=wloc.ap()), sems["qe"], 16)
        allw = [(s, s.n) for s in sems.values() if s.n > 0]
        em("gpsimd", allw, None)
        B.run()
        es.close()
        return nc

    def dequant_unit(pk, sct, kind, wt):
        qb = wt % 2
        em("gpsimd", gw(("qpk", kind, qb)), (lambda pk, wt, qb: lambda e: e.dma_start(
            out=q_pk[qb][:],
            in_=pk.ap().rearrange("(w p) c -> w p c", p=128)[wt]))(pk, wt, qb), qdq, 16)
        pk_g = qdq.n
        for kq in range(8):
            em("gpsimd", gw(("qst", kind, qb)) if kq == 0 else (), (lambda sct, wt, kq, qb: lambda e: e.dma_start(
                out=q_st[qb][16 * kq:16 * (kq + 1), :],
                in_=rap(sct, (wt * 8 + kq) * 2048, (0, 16), (1, 2048))))(sct, wt, kq, qb), qst, 16)
        st_g = qst.n
        for hi in range(2):
            wb = hi
            if hi:
                em("vector", [(qdq, pk_g)], (lambda qb: lambda e: e.tensor_scalar(
                    q_u8[:], q_pk[qb][:], 4, 15,
                    ALU.logical_shift_right, ALU.bitwise_and))(qb), ve, 1)
            else:
                em("vector", [(qdq, pk_g)], (lambda qb: lambda e: e.tensor_single_scalar(
                    q_u8[:], q_pk[qb][:], 15, ALU.bitwise_and))(qb), ve, 1)
            if hi:
                gs(("qpk", kind, qb), ve)
            em("vector", (), lambda e: e.tensor_copy(q_vf[:], q_u8[:]), ve, 1)
            em("vector", (), lambda e: e.tensor_single_scalar(q_b3[:], q_vf[:], 8.0, ALU.is_ge), ve, 1)
            em("vector", (), lambda e: e.scalar_tensor_tensor(q_vf[:], q_b3[:], -8.0, q_vf[:], ALU.mult, ALU.add), ve, 1)
            m_v2 = ve.n
            em("scalar", [(ve, m_v2)], lambda e: e.activation(q_r1[:], q_vf[:], AF.Relu, bias=kcs[:, 4:5]), scm, 1)
            em("scalar", (), lambda e: e.activation(q_r2[:], q_vf[:], AF.Relu, bias=kcs[:, 5:6]), scm, 1)
            r_s = scm.n
            em("vector", [(scm, r_s)], lambda e: e.tensor_add(q_r1[:], q_vf[:], q_r1[:]), ve, 1)
            em("vector", (), lambda e: e.scalar_tensor_tensor(q_r1[:], q_r1[:], 0.5, q_r2[:], ALU.mult, ALU.add), ve, 1)
            em("vector", (), lambda e: e.tensor_scalar(q_b3[:], q_b3[:], -2.0, 1.0, ALU.mult, ALU.add), ve, 1)
            em("vector", [(qst, st_g)], (lambda qb: lambda e: e.tensor_mul(q_b3[:], q_b3[:], q_st[qb][:]))(qb), ve, 1)
            if hi:
                gs(("qst", kind, qb), ve)
            em("vector", gw(("qw", wb)), (lambda wb: lambda e: e.tensor_mul(
                q_w[wb][:], q_r1[:], q_b3[:]))(wb), ve, 1)
            w_v = ve.n
            if kind < 2:
                kc = wt + 8 * hi
                em("gpsimd", [(ve, w_v)], (lambda kind, kc, wb: lambda e: e.dma_start(
                    out=rap(wloc, kind * 2048 * 2048 + kc * 128 * 256,
                            (256, 128), (16 * 128 * 256, 8), (1, 256)),
                    in_=q_w[wb][:].rearrange("p (a c) -> p a c", a=8)))(kind, kc, wb), qw8[wb], 16)
            else:
                nr = 128 * wt if hi == 0 else 1024 + 128 * wt
                em("gpsimd", [(ve, w_v)], (lambda nr, wb: lambda e: e.dma_start(
                    out=rap(wloc, 2 * 2048 * 2048 + nr * 256,
                            (256, 128), (2048 * 256, 8), (1, 256)),
                    in_=q_w[wb][:].rearrange("p (a c) -> p a c", a=8)))(nr, wb), qw8[wb], 16)
            gs(("qw", wb), sems["qw8%d" % wb])

    dq_units = ([(gpk, gsc, 0, wt) for wt in range(8)] +
                [(upk, usc, 1, wt) for wt in range(8)] +
                [(dpk, dsc, 2, wt) for wt in range(8)])
    dq_i = [0]

    def dq_hook():
        if dq_i[0] < len(dq_units):
            dequant_unit(*dq_units[dq_i[0]])
            dq_i[0] += 1

    if stage <= 1:
        return finish()

    em("vector", (), lambda e: e.memset(s_one[:], 1.0), ve, 1)
    scan_pass(pass2=False, hook=(dq_hook if stage >= 3 else None))
    p1_v = ve.n

    em("gpsimd", [(ve, p1_v), (scm, dc_s), (scm, scm.n), (gpo, gpo.n)], lambda e: e.dma_start(
        out=rap(e0loc, 0, (1024, 128), (1, 512)), in_=e0b[:]), qe, 16)
    em("gpsimd", (), lambda e: e.dma_start(
        out=rap(e0loc, 512, (1024, 128), (1, 512)), in_=dcal[:]), qe, 16)
    em("gpsimd", [(qe, qe.n)], lambda e: e.collective_compute(
        "AllGather", ALU.bypass, replica_groups=[list(range(NCORE))],
        ins=[e0loc.ap().opt()], outs=[e0g.ap().opt()]), cc, 1)
    cc_e0 = cc.n
    em("vector", [(cc, cc_e0)], lambda e: e.memset(hin[:], 0.0), ve, 1)
    for r in range(7):
        em("gpsimd", [(cc, cc_e0), (ve, ve.n)], (lambda r: lambda e: e.dma_start(
            out=s_e[0][:], in_=rap(e0g, r * 128 * 1024 + 0, (1024, 128), (1, 512))))(r), qe, 16)
        em("gpsimd", (), (lambda r: lambda e: e.dma_start(
            out=s_e[1][:], in_=rap(e0g, r * 128 * 1024 + 512, (1024, 128), (1, 512))))(r), qe, 16)
        rg = qe.n
        em("vector", [(qe, rg)], (lambda r: lambda e: e.tensor_scalar(
            s_e[1][:], s_e[1][:], mskd[:, r:r + 1], mskdc[:, r:r + 1], ALU.mult, ALU.add))(r), ve, 1)
        em("vector", (), lambda e: e.tensor_mul(hin[:], hin[:], s_e[1][:]), ve, 1)
        em("vector", (), (lambda r: lambda e: e.tensor_scalar(
            s_e[0][:], s_e[0][:], mske[:, r:r + 1], None, ALU.mult))(r), ve, 1)
        em("vector", (), lambda e: e.tensor_add(hin[:], hin[:], s_e[0][:]), ve, 1)

    pre_reps2 = build_reps(0, with_c=True, bw=[(ve, p1_v), (gpo, gpo.n)])
    if stage >= 3:
        while dq_i[0] < len(dq_units):
            dq_hook()
        em("gpsimd", [(qw8[0], qw8[0].n), (qw8[1], qw8[1].n)], lambda e: e.collective_compute(
            "AllGather", ALU.bypass, replica_groups=[list(range(NCORE))],
            ins=[wloc.ap().opt()], outs=[wgth.ap().opt()]), cc, 1)
    cc_w = cc.n

    if stage <= 2:
        return finish()

    if stage <= 3:
        return finish()

    scan_pass(pass2=True, pre_reps=pre_reps2)
    p2_y = qy.n

    if stage <= 4:
        return finish()

    first_ffn_load = [True]
    for t in range(4):
        em("sync", [(qy, p2_y), (pe, pe.n), (qy2, qy2.n), (qo[0], qo[0].n), (qo[1], qo[1].n),
                    (qx2p[0], qx2p[0].n)] + gw("oy2"), (lambda t: lambda e: e.dma_start(
            out=o_y2[:].rearrange("p (c x) -> p c x", c=ICH),
            in_=rap(y2_sp, t * TT, (T, 128), (128 * T, ICH), (1, TT))))(t), qy2, 16)
        y2_w = (qy2, qy2.n)
        for ms in range(4):
            sb_i = ms % 2
            em("sync", gw(("osl", sb_i)), (lambda ms, sb_i: lambda e: e.dma_start(
                out=o_slab[sb_i][:].rearrange("p (a b) -> p a b", b=512),
                in_=rap(wob, ms * ICH * 128 * 512, (512, 128), (128 * 512, ICH), (1, 512))))(ms, sb_i),
               qsl[sb_i], 16)
            slab_w = (qsl[sb_i], qsl[sb_i].n)
            for mc in range(4):
                m = ms * 4 + mc
                bank = mc % 2
                xp_s = m % 2
                em("sync", gw(("oxp", xp_s)), (lambda t, m, xp_s: lambda e: e.dma_start(
                    out=o_xp[xp_s][:],
                    in_=xin.ap()[:, TM + t * TT:TM + (t + 1) * TT]
                        .rearrange("(c p) x -> c p x", p=128)[m]))(t, m, xp_s), qxp[xp_s], 16)
                xp_w = (qxp[xp_s], qxp[xp_s].n)

                def opmm(mc, bank, sb_i):
                    def go(e):
                        for kc in range(ICH):
                            ins = nc.tensor.matmul(
                                psum[bank][:, 0:TT],
                                o_slab[sb_i][:, kc * 512 + mc * 128:kc * 512 + mc * 128 + 128],
                                o_y2[:, kc * TT:(kc + 1) * TT],
                                start=(kc == 0), stop=(kc == ICH - 1))
                        return ins
                    return go
                em("tensor", [slab_w, y2_w] + gw(("pb", bank)), opmm(mc, bank, sb_i), pe, 1)
                if mc == 3:
                    gs(("osl", sb_i), pe)
                em("vector", [(pe, pe.n), xp_w] + gw("ox2"), (lambda m, bank, xp_s: lambda e: e.tensor_add(
                    o_x2[:, m * TT:(m + 1) * TT], o_xp[xp_s][:],
                    psum[bank][:, 0:TT]))(m, bank, xp_s), ve, 1)
                gs(("pb", bank), ve)
                gs(("oxp", xp_s), ve)
        x2_v = ve.n
        gs("oy2", pe)
        em("gpsimd", [(ve, x2_v), (qx2, 16 * t)], (lambda t: lambda e: e.dma_start(
            out=rap(x2_sp, t * TT, (T, 128), (128 * T, KCH), (1, TT)),
            in_=o_x2[:].rearrange("p (c x) -> p c x", c=KCH)))(t), qx2, 16)
        gs("ox2", qx2)
        em("vector", [(ve, x2_v)] + gw("x2ln"), lambda e: e.tensor_copy(o_x2ln[:], o_x2[:]), ve, 1)
        ln_stats_apply(o_x2ln, o_xsq, o_x2ln, TT, [(ve, ve.n)], o_bc1, o_bc2)
        x2ln_v = ve.n

        for _b in range(2):
            for _k in ("fg", "fu", "fd"):
                guard[(_k, _b)] = guard.get((_k, _b), []) + [(pe, pe.n)]
        dbg_gate = []
        for ih in range(2):
            for r in range(4 * ih, 4 * ih + 4):
                for ncc in range(8):
                    nb = (r * 8 + ncc) % 2
                    lw = gw(("fg", nb)) + dbg_gate
                    if first_ffn_load[0]:
                        lw = lw + [(cc, cc_w)]
                        first_ffn_load[0] = False
                    em("sync", lw, (lambda r, ncc, nb: lambda e: e.dma_start(
                        out=f_g[nb][:].rearrange("p (a c) -> p a c", a=16),
                        in_=rap(wgth, r * 6144 * 2048 + ncc * 16 * 128 * 256,
                                (256, 128), (128 * 256, 16), (1, 256))))(r, ncc, nb), qfg[nb], 16)
                    em("sync", gw(("fu", nb)), (lambda r, ncc, nb: lambda e: e.dma_start(
                        out=f_u[nb][:].rearrange("p (a c) -> p a c", a=16),
                        in_=rap(wgth, r * 6144 * 2048 + 2048 * 2048 + ncc * 16 * 128 * 256,
                                (256, 128), (128 * 256, 16), (1, 256))))(r, ncc, nb), qfu[nb], 16)
                    gq = [(qfg[nb], qfg[nb].n), (qfu[nb], qfu[nb].n)]
                    for sub in range(2):
                        lnch = (r - 4 * ih) * 16 + ncc * 2 + sub
                        gbank = 4 + sub
                        ubank = 6 + sub

                        def gumm(nb, sub, gbank, ubank):
                            def go(e):
                                for kc in range(KCH):
                                    nc.tensor.matmul(psum[gbank][:, 0:TT],
                                                     f_g[nb][:, kc * 256 + sub * 128:kc * 256 + sub * 128 + 128],
                                                     o_x2ln[:, kc * TT:(kc + 1) * TT],
                                                     start=(kc == 0), stop=(kc == KCH - 1))
                                for kc in range(KCH):
                                    ins = nc.tensor.matmul(psum[ubank][:, 0:TT],
                                                           f_u[nb][:, kc * 256 + sub * 128:kc * 256 + sub * 128 + 128],
                                                           o_x2ln[:, kc * TT:(kc + 1) * TT],
                                                           start=(kc == 0), stop=(kc == KCH - 1))
                                return ins
                            return go
                        em("tensor", gq + [(ve, x2ln_v)] + gw(("pb", gbank), ("pb", ubank)),
                           gumm(nb, sub, gbank, ubank), pe, 1)
                        gu_p = pe.n
                        if sub == 1:
                            gs(("fg", nb), pe)
                            gs(("fu", nb), pe)
                        em("scalar", [(pe, gu_p)] + gw(("fgg", sub)), (lambda gbank, sub: lambda e: e.activation(
                            f_gg[sub][:], psum[gbank][:, 0:TT], AF.Gelu_apprx_tanh))(gbank, sub), scm, 1)
                        gs(("pb", gbank), scm)
                        em("vector", [(scm, scm.n)], (lambda lnch, ubank, sub: lambda e: e.tensor_mul(
                            f_int[:, lnch * TT:(lnch + 1) * TT], f_gg[sub][:],
                            psum[ubank][:, 0:TT]))(lnch, ubank, sub), ve, 1)
                        gs(("pb", ubank), ve)
                        gs(("fgg", sub), ve)
            int_v = ve.n
            gs("x2ln", pe)
            if debug and t == 0 and ih == 0:
                em("gpsimd", [(ve, int_v)], lambda e: e.dma_start(
                    out=fdbg1.ap()[:, 0:64 * TT], in_=f_int[:, 0:64 * TT]), sems["qe"], 16)
                em("gpsimd", (), lambda e: e.dma_start(
                    out=fdbg3.ap()[:, 0:KCH * TT], in_=o_x2ln[:]), sems["qe"], 16)
                dbg_gate = [(sems["qe"], sems["qe"].n)]

            if debug and t == 0 and ih == 1:
                em("gpsimd", [(ve, ve.n)], lambda e: e.dma_start(
                    out=fdbg2.ap()[:, 0:KCH * TT], in_=f_acc[:]), sems["qe"], 16)
                guard["oy2"] = guard.get("oy2", []) + [(sems["qe"], sems["qe"].n)]
            for d2c in range(8):
                if ih == 1:
                    for sub in range(2):
                        dch = d2c * 2 + sub
                        em("gpsimd", gw(("x2p", sub), "obc") + [(qx2, 16 * (t + 1))], (lambda dch, t, sub: lambda e: e.dma_start(
                            out=f_x2p[sub][:],
                            in_=rap(x2_sp, dch * 128 * T + t * TT, (T, 128), (1, TT))))(dch, t, sub), qx2p[sub], 16)
                    x2p_w = [(qx2p[0], qx2p[0].n), (qx2p[1], qx2p[1].n)]
                for r in range(4 * ih, 4 * ih + 4):
                    db = r % 2
                    em("sync", gw(("fd", db)), (lambda d2c, r, db: lambda e: e.dma_start(
                        out=f_d[db][:].rearrange("p (a c) -> p a c", a=16),
                        in_=rap(wgth, r * 6144 * 2048 + 2 * 2048 * 2048 + d2c * 2048 * 256,
                                (256, 128), (128 * 256, 16), (1, 256))))(d2c, r, db), qfd[db], 16)
                    dq_w = (qfd[db], qfd[db].n)

                    def dmm(r, db, ih):
                        def go(e):
                            for nct in range(16):
                                lnch = (r - 4 * ih) * 16 + nct
                                for sub in range(2):
                                    ins = nc.tensor.matmul(
                                        psum[sub][:, 0:TT],
                                        f_d[db][:, nct * 256 + sub * 128:nct * 256 + sub * 128 + 128],
                                        f_int[:, lnch * TT:(lnch + 1) * TT],
                                        start=(r == 4 * ih and nct == 0),
                                        stop=(r == 4 * ih + 3 and nct == 15))
                            return ins
                        return go
                    dw = [dq_w, (ve, int_v)]
                    if r == 4 * ih:
                        dw += gw(("pbd", 0), ("pbd", 1))
                    em("tensor", dw, dmm(r, db, ih), pe, 1)
                    gs(("fd", db), pe)
                d_p = pe.n
                if False and debug and t == 1 and ih == 1 and d2c == 0:
                    em("vector", [(pe, d_p)], lambda e: e.tensor_copy(o_bc1[:].bitcast(F32)[:, 0:256], psum[0][:, 0:256]), ve, 1)
                    em("gpsimd", [(ve, ve.n)], lambda e: e.dma_start(out=fdbg4.ap()[:, 0:256], in_=o_bc1[:].bitcast(F32)[:, 0:256]), sems["qe"], 16)
                    em("gpsimd", (), lambda e: e.dma_start(out=fdbg5.ap()[:, 0:TT], in_=f_acc[:, 0:TT]), sems["qe"], 16)
                    em("gpsimd", [(qx2p[0], qx2p[0].n)], lambda e: e.dma_start(out=fdbg6.ap()[:, 0:TT], in_=f_x2p[0][:]), sems["qe"], 16)
                    em("gpsimd", (), lambda e: e.dma_start(out=fdbg7.ap()[:, 0:64 * TT], in_=f_int[:, 0:64 * TT]), sems["qe"], 16)
                for sub in range(2):
                    dch = d2c * 2 + sub
                    if ih == 0:
                        em("vector", [(pe, d_p)], (lambda dch, sub: lambda e: e.tensor_copy(
                            f_acc[:, dch * TT:(dch + 1) * TT], psum[sub][:, 0:TT]))(dch, sub), ve, 1)
                        gs(("pbd", sub), ve)
                    else:
                        em("vector", [(pe, d_p)] + x2p_w + [(qo[sub], qo[sub].n)],
                           (lambda dch, sub: lambda e: e.tensor_add(
                               f_o[sub][:], f_acc[:, dch * TT:(dch + 1) * TT],
                               psum[sub][:, 0:TT]))(dch, sub), ve, 1)
                        em("vector", (), (lambda sub: lambda e: e.tensor_add(
                            f_o[sub][:], f_o[sub][:], f_x2p[sub][:]))(sub), ve, 1)
                        gs(("pbd", sub), ve)
                        em("gpsimd", [(ve, ve.n)], (lambda dch, t, sub: lambda e: e.dma_start(
                            out=rap(out, dch * 128 * T + t * TT, (T, 128), (1, TT)),
                            in_=f_o[sub][:]))(dch, t, sub), qo[sub], 16)
                        gs(("x2p", sub), ve)

    return finish()


def _perm_k():
    return np.concatenate([np.arange(1024) * 2, np.arange(1024) * 2 + 1])


def _perm_n():
    p = np.empty(NMLP, np.int64)
    j = np.arange(1024)
    for cr in range(8):
        p[2048 * cr + j] = 2 * (1024 * cr + j)
        p[2048 * cr + 1024 + j] = 2 * (1024 * cr + j) + 1
    return p


def _pack_u8(p32):
    return np.ascontiguousarray(np.asarray(p32).astype(np.int32).view(np.uint8)[:, 0::4])


def _host_prep(inp, T):
    TM = T // 4
    pk = _perm_k()
    pn = _perm_n()

    xr = np.asarray(inp["x"], np.float32).reshape(-1, DM)
    x = np.ascontiguousarray(xr.T[pk])
    w1t_h = np.ascontiguousarray(np.asarray(inp["w1"], np.float32).T[pk])
    xp = np.asarray(inp["wx"], np.float32)
    wxt_h = np.zeros((DI, 176), np.float32)
    wxt_h[:, :128] = xp[:128].T
    wxt_h[:, 128:144] = xp[128:144].T
    wxt_h[:, 160:176] = xp[144:160].T
    wdt_h = np.ascontiguousarray(np.asarray(inp["wd"], np.float32).T)
    wot_h = np.ascontiguousarray(np.asarray(inp["wo"], np.float32).T[:, pk])

    def chl(v, n):
        return np.ascontiguousarray(np.asarray(v, np.float32).reshape(n, 128).T)

    convw_h = np.ascontiguousarray(np.asarray(inp["cw"], np.float32)
                                   .reshape(ICH, 128, DC).transpose(1, 0, 2).reshape(128, ICH * DC))
    dtb_h = chl(inp["db"], ICH)
    dvec_h = chl(inp["dv"], ICH)
    alog_h = np.ascontiguousarray(np.asarray(inp["al"], np.float32)
                                  .reshape(ICH, 128, DN).transpose(1, 0, 2).reshape(128, ICH * DN))

    g8T = np.ascontiguousarray(_pack_u8(inp["gp"])[pn].T)
    u8T = np.ascontiguousarray(_pack_u8(inp["up"])[pn].T)
    d8T = np.ascontiguousarray(_pack_u8(inp["dp"])[pk].T)
    gsT = np.ascontiguousarray(np.asarray(inp["gs"], np.float32)[pn].T)
    usT = np.ascontiguousarray(np.asarray(inp["us"], np.float32)[pn].T)
    dsT = np.ascontiguousarray(np.asarray(inp["ds"], np.float32)[pk].T)

    idm_h = np.eye(128, dtype=np.float32)
    ones_h = np.ones((128, 1), np.float32)
    kc_h = np.zeros((128, 8), np.float32)
    kc_h[:, 0] = 1e-5
    kc_h[:, 1] = -0.5
    kc_h[:, 2] = -1.0
    kc_h[:, 3] = 1.0
    kc_h[:, 4] = -4.0
    kc_h[:, 5] = -6.0

    ncores = NCORE
    tpc = x.shape[1] // ncores
    assert tpc >= T, (tpc, T)

    in_maps = []
    for cr in range(ncores):
        t0 = cr * tpc
        xw = np.zeros((DM, TM + T), np.float32)
        if cr % 4 != 0:
            xw[:, :TM] = x[:, t0 - TM:t0]
        xw[:, TM:] = x[:, t0:t0 + T]
        mdv = np.zeros((128, 8), np.float32)
        mev = np.zeros((128, 8), np.float32)
        for r in range(4 * (cr // 4), cr):
            mdv[:, r] = 1.0
            mev[:, r] = 1.0
        in_maps.append({
            "xin": np.ascontiguousarray(xw),
            "w1t": w1t_h, "wxt": wxt_h, "wdt": wdt_h, "wot": wot_h,
            "convw": convw_h, "dtb": dtb_h, "dvec": dvec_h, "alog": alog_h,
            "maskd": mdv, "maskdc": np.ascontiguousarray(1.0 - mdv), "maske": mev,
            "idm": idm_h, "onescol": ones_h, "kcst": kc_h,
            "gpk": np.ascontiguousarray(g8T[:, cr * 2048:(cr + 1) * 2048]),
            "upk": np.ascontiguousarray(u8T[:, cr * 2048:(cr + 1) * 2048]),
            "dpk": np.ascontiguousarray(d8T[cr * 1024:(cr + 1) * 1024]),
            "gsc": np.ascontiguousarray(gsT[:, cr * 2048:(cr + 1) * 2048]),
            "usc": np.ascontiguousarray(usT[:, cr * 2048:(cr + 1) * 2048]),
            "dsc": np.ascontiguousarray(dsT[64 * cr:64 * (cr + 1)]),
        })
    return in_maps


def kernel(L0_attn_hidden_states, L0_attn_in_proj_weight, L0_attn_conv1d_weight,
           L0_attn_x_proj_weight, L0_attn_dt_proj_weight, L0_attn_dt_proj_bias,
           L0_attn_A_log, L0_attn_D, L0_attn_out_proj_weight,
           L0_ffn_gate_W_packed, L0_ffn_gate_scales, L0_ffn_up_W_packed,
           L0_ffn_up_scales, L0_ffn_down_W_packed, L0_ffn_down_scales):
    T = 2048
    inp = {"x": L0_attn_hidden_states, "w1": L0_attn_in_proj_weight,
           "cw": L0_attn_conv1d_weight, "wx": L0_attn_x_proj_weight,
           "wd": L0_attn_dt_proj_weight, "db": L0_attn_dt_proj_bias,
           "al": L0_attn_A_log, "dv": L0_attn_D, "wo": L0_attn_out_proj_weight,
           "gp": L0_ffn_gate_W_packed, "gs": L0_ffn_gate_scales,
           "up": L0_ffn_up_W_packed, "us": L0_ffn_up_scales,
           "dp": L0_ffn_down_W_packed, "ds": L0_ffn_down_scales}
    in_maps = _host_prep(inp, T)
    nc = build_nc(T=T)
    res = run_bass_kernel_spmd(nc, in_maps, core_ids=list(range(NCORE)))
    outs = [np.asarray(res.results[cr]["out"]) for cr in range(NCORE)]
    full = np.concatenate(outs, axis=1)
    inv_pk = np.argsort(_perm_k())
    return np.ascontiguousarray(full[inv_pk].T).reshape(2, 8192, DM).astype(np.float32)


# version 10
# speedup vs baseline: 1.0944x; 1.0944x over previous
import numpy as np
from contextlib import ExitStack

import concourse.bass as bass
import concourse.mybir as mybir
from concourse.bass_utils import run_bass_kernel_spmd

F32 = mybir.dt.float32
BF16 = mybir.dt.bfloat16
U8 = mybir.dt.uint8
AF = mybir.ActivationFunctionType
ALU = mybir.AluOpType

DM = 2048
DI = 4096
DN = 16
DR = 128
DC = 4
NMLP = 16384
NCORE = 8
KCH = DM // 128
ICH = DI // 128
NCH = NMLP // 128


def rap(t, off, *dims):
    return bass.AP(t, int(off), [list(d) for d in dims])


class Sem:
    __slots__ = ("h", "n")

    def __init__(self, h):
        self.h = h
        self.n = 0


class Builder:
    def __init__(self, nc):
        self.nc = nc
        self.streams = {k: [] for k in ("sync", "vector", "scalar", "tensor", "gpsimd")}
        self.own = {}

    def emit(self, eng, waits=(), fn=None, inc=None, inck=1):
        waits = [(s, int(v)) for (s, v) in waits]
        own = self.own.get(eng)
        if own is not None and fn is not None and not (eng == "gpsimd" and inc is not None):
            waits.append((own, own.n))
            if inc is None:
                inc = own
                inck = 1
        if inc is not None:
            inc.n += inck

        def go(e):
            for (s, v) in waits:
                if v > 0:
                    e.wait_ge(s.h, v)
            ins = fn(e) if fn is not None else None
            if inc is not None:
                assert ins is not None, "inc requires an instruction"
                ins.then_inc(inc.h, inck)

        self.streams[eng].append(go)

    def run(self):
        nc = self.nc
        with nc.Block() as block:
            @block.sync
            def _(e):
                for f in self.streams["sync"]:
                    f(e)

            @block.vector
            def _(e):
                for f in self.streams["vector"]:
                    f(e)

            @block.scalar
            def _(e):
                for f in self.streams["scalar"]:
                    f(e)

            @block.tensor
            def _(e):
                for f in self.streams["tensor"]:
                    f(e)

            @block.gpsimd
            def _(e):
                for f in self.streams["gpsimd"]:
                    f(e)


def build_nc(T=2048, debug=False, stage=5):
    assert T % 8 == 0
    TM = T // 4
    TT = T // 4
    HALF = T // 2
    assert TT <= 512 and TM >= 8
    UN = ICH * DN

    nc = bass.Bass("TRN2", target_bir_lowering=False)

    def din(name, shape, dt=F32):
        return nc.dram_tensor(name, shape, dt, kind="ExternalInput")

    xin = din("xin", [DM, TM + T])
    w1t = din("w1t", [DM, 2 * DI])
    wxt = din("wxt", [DI, 176])
    wdt = din("wdt", [DR, DI])
    wot = din("wot", [DI, DM])
    convw = din("convw", [128, ICH * DC])
    dtb = din("dtb", [128, ICH])
    dvec = din("dvec", [128, ICH])
    alog = din("alog", [128, UN])
    maskd = din("maskd", [128, 8])
    maskdc = din("maskdc", [128, 8])
    maske = din("maske", [128, 8])
    idm = din("idm", [128, 128])
    onescol = din("onescol", [128, 1])
    kcst = din("kcst", [128, 8])
    gpk = din("gpk", [1024, 2048], U8)
    upk = din("upk", [1024, 2048], U8)
    dpk = din("dpk", [1024, 2048], U8)
    gsc = din("gsc", [64, 2048])
    usc = din("usc", [64, 2048])
    dsc = din("dsc", [64, 2048])

    out = nc.dram_tensor("out", [DM, T], F32, kind="ExternalOutput")

    dkind = "ExternalOutput" if debug else "Internal"

    def dint(name, shape, dt, dbg=False):
        return nc.dram_tensor(name, shape, dt, kind=(dkind if dbg else "Internal"))

    w1b = dint("w1b", [16, KCH, 128, 512], BF16)
    wob = dint("wob", [4, ICH, 128, 512], BF16)
    sz_sp = dint("sz_sp", [ICH, 128, T], BF16, dbg=True)
    xc_sp = dint("xc_sp", [ICH, 128, T], BF16, dbg=True)
    dl_sp = dint("dl_sp", [ICH, 128, T], BF16, dbg=True)
    du_sp = dint("du_sp", [ICH, 128, T], BF16, dbg=True)
    bmcm = dint("bmcm", [64, T], F32, dbg=True)
    muv = dint("muv", [2, 512], F32)
    e0loc = dint("e0loc", [128, 1024], F32)
    e0g = dint("e0g", [1024, 1024], F32)
    e0dbg = dint("e0dbg", [128, 1024], F32, dbg=True) if debug else None
    y2_sp = dint("y2_sp", [ICH, 128, T], BF16, dbg=True)
    x2_sp = dint("x2_sp", [KCH, 128, T], F32, dbg=True)
    wloc = dint("wloc", [6144, 2048], BF16)
    wldbg = dint("wldbg", [6144, 2048], BF16, dbg=True) if debug else None
    fdbg1 = dint("fdbg1", [128, 64 * 512], BF16, dbg=True) if debug else None
    fdbg2 = dint("fdbg2", [128, 16 * 512], F32, dbg=True) if debug else None
    fdbg3 = dint("fdbg3", [128, 16 * 512], BF16, dbg=True) if debug else None
    fdbg4 = dint("fdbg4", [128, 512], F32, dbg=True) if debug else None
    fdbg5 = dint("fdbg5", [128, 512], F32, dbg=True) if debug else None
    fdbg6 = dint("fdbg6", [128, 512], F32, dbg=True) if debug else None
    fdbg7 = dint("fdbg7", [128, 64 * 512], BF16, dbg=True) if debug else None
    wgth = dint("wgth", [8, 6144, 2048], BF16)

    ARENA = 206848
    es = ExitStack()
    es.__enter__()
    psum = [es.enter_context(nc.psum_tensor(f"pb{i}", [128, 512], F32)) for i in range(8)]
    backing = es.enter_context(nc.sbuf_tensor("arena", [128, ARENA], U8))
    base_addr = nc.lookup_mloc(backing).addr

    def sb(name, shape, dt, off):
        nbytes = int(np.prod(shape[1:])) * mybir.dt.size(dt)
        assert off + nbytes <= ARENA, (name, off, nbytes)
        return nc.alloc_sbuf_tensor_at(name, shape, dt, offset=base_addr + int(off))

    oR = [0]

    def resv(nbytes):
        off = oR[0]
        oR[0] = (oR[0] + int(nbytes) + 63) & ~63
        return off

    aneg = sb("aneg", [128, UN], F32, resv(UN * 4))
    apos = sb("apos", [128, UN], F32, resv(UN * 4))
    cw = sb("cw", [128, ICH * DC], F32, resv(ICH * DC * 4))
    dtbs = sb("dtbs", [128, ICH], F32, resv(ICH * 4))
    dvs = sb("dvs", [128, ICH], F32, resv(ICH * 4))
    mskd = sb("mskd", [128, 8], F32, resv(32))
    mskdc = sb("mskdc", [128, 8], F32, resv(32))
    mske = sb("mske", [128, 8], F32, resv(32))
    kcs = sb("kcs", [128, 8], F32, resv(32))
    idmb = sb("idmb", [128, 128], BF16, resv(256))
    onesb = sb("onesb", [128, 1], BF16, resv(64))
    tdp = sb("tdp", [128, ICH * 4], F32, resv(ICH * 4 * 4))
    td = sb("td", [128, ICH], F32, resv(ICH * 4))
    stat = sb("stat", [128, 1536], F32, resv(1536 * 4))
    e0a = sb("e0a", [128, UN], F32, resv(UN * 4))
    e0b = sb("e0b", [128, UN], F32, resv(UN * 4))
    dcal = sb("dcal", [128, UN], F32, resv(UN * 4))
    hin = sb("hin", [128, UN], F32, resv(UN * 4))
    hcar = sb("hcar", [128, UN], F32, resv(UN * 4))
    SCR = oR[0]

    def mk_scr():
        cur = [0]

        def f(nbytes):
            off = cur[0] + SCR
            cur[0] = (cur[0] + int(nbytes) + 63) & ~63
            return off
        return cur, f

    c, f = mk_scr()
    m_x = sb("m_x", [128, max(KCH * TM, 4096)], BF16, f(max(KCH * TM, 4096) * 2))
    m_xln = sb("m_xln", [128, KCH * TM], BF16, f(KCH * TM * 2))
    m_slab = [sb(f"m_slab{i}", [128, KCH * 512], BF16, f(KCH * 512 * 2)) for i in range(2)]
    m_xpre = sb("m_xpre", [128, ICH * (TM + 3)], BF16, f(ICH * (TM + 3) * 2))
    m_xc = sb("m_xc", [128, ICH * TM], BF16, f(ICH * TM * 2))
    wxtb = sb("wxtb", [128, ICH * 176], BF16, f(ICH * 176 * 2))
    wdtb = sb("wdtb", [128, DI], BF16, f(DI * 2))
    m_sz = [sb(f"m_sz{i}", [128, TM], BF16, f(TM * 2)) for i in range(2)]
    m_cv = [sb(f"m_cv{i}", [128, TM], F32, f(TM * 4)) for i in range(2)]
    m_dt = sb("m_dt", [128, TM], BF16, f(TM * 2))
    m_bc = sb("m_bc", [64, TM], F32, f(TM * 4))
    m_sp1 = sb("m_sp1", [128, TM], F32, f(TM * 4))
    m_sp2 = sb("m_sp2", [128, TM], F32, f(TM * 4))
    m_dl = [sb(f"m_dl{i}", [128, TM], BF16, f(TM * 2)) for i in range(2)]
    m_du = [sb(f"m_du{i}", [128, TM], BF16, f(TM * 2)) for i in range(2)]
    m_bc1 = sb("m_bc1", [128, 512], BF16, f(512 * 2))
    m_bc2 = sb("m_bc2", [128, 512], BF16, f(512 * 2))
    m_top = c[0]

    c, f = mk_scr()
    s_brep = sb("s_brep", [128, DN * HALF], BF16, f(DN * HALF * 2))
    s_crep = sb("s_crep", [128, DN * HALF], BF16, f(DN * HALF * 2))
    s_dl = [sb(f"s_dl{i}", [128, HALF], BF16, f(HALF * 2)) for i in range(2)]
    s_du = [sb(f"s_du{i}", [128, HALF], BF16, f(HALF * 2)) for i in range(2)]
    s_xc = [sb(f"s_xc{i}", [128, HALF], BF16, f(HALF * 2)) for i in range(2)]
    s_sz = [sb(f"s_sz{i}", [128, HALF], BF16, f(HALF * 2)) for i in range(2)]
    s_a = [sb(f"s_a{i}", [128, HALF], BF16, f(HALF * 2)) for i in range(2)]
    s_u = [sb(f"s_u{i}", [128, HALF], BF16, f(HALF * 2)) for i in range(2)]
    s_h = [sb(f"s_h{i}", [128, HALF], BF16, f(HALF * 2)) for i in range(2)]
    s_tmp = [sb(f"s_tmp{i}", [128, HALF], BF16, f(HALF * 2)) for i in range(2)]
    s_y1 = sb("s_y1", [128, TT], F32, f(TT * 4))
    s_y2 = sb("s_y2", [128, TT], BF16, f(TT * 2))
    s_e = [sb(f"s_e{i}", [128, 512], F32, f(512 * 4)) for i in range(2)]
    s_cd = sb("s_cd", [128, HALF], F32, f(HALF * 4))
    s_one = sb("s_one", [128, HALF], BF16, f(HALF * 2))
    s_bx = sb("s_bx", [128, 32], F32, f(128))
    p_a, p_u, p_h, p_t = s_a, s_u, s_h, s_tmp
    s_top = c[0]

    c, f = mk_scr()
    c[0] = max(s_top, 16384 + ((KCH * 512 * 2 + 63) & ~63))
    q_pk = [sb(f"q_pk{i}", [128, 2048], U8, f(2048)) for i in range(2)]
    q_u8 = sb("q_u8", [128, 2048], U8, f(2048))
    q_st = [sb(f"q_st{i}", [128, 2048], F32, f(2048 * 4)) for i in range(2)]
    q_vf = sb("q_vf", [128, 2048], F32, f(2048 * 4))
    q_b3 = sb("q_b3", [128, 2048], F32, f(2048 * 4))
    q_r1 = sb("q_r1", [128, 2048], F32, f(2048 * 4))
    q_r2 = sb("q_r2", [128, 2048], F32, f(2048 * 4))
    q_w = [sb(f"q_w{i}", [128, 2048], BF16, f(2048 * 2)) for i in range(2)]
    q_top = c[0]

    c, f = mk_scr()
    o_x2ln = sb("o_x2ln", [128, KCH * TT], BF16, f(KCH * TT * 2))
    f_base = c[0]
    o_y2 = sb("o_y2", [128, ICH * TT], BF16, f(ICH * TT * 2))
    o_xsq = sb("o_xsq", [128, KCH * TT], BF16, f(KCH * TT * 2))
    o_slab = [sb(f"o_slab{i}", [128, ICH * 512], BF16, f(ICH * 512 * 2)) for i in range(2)]
    o_x2 = sb("o_x2", [128, KCH * TT], F32, f(KCH * TT * 4))
    o_xp = [sb(f"o_xp{i}", [128, TT], F32, f(TT * 4)) for i in range(2)]
    o_bc1 = sb("o_bc1", [128, 512], BF16, f(512 * 2))
    o_bc2 = sb("o_bc2", [128, 512], BF16, f(512 * 2))
    o_top = c[0]

    c, f = mk_scr()
    c[0] = f_base
    f_int = sb("f_int", [128, (NCH // 2) * TT], BF16, f((NCH // 2) * TT * 2))
    f_acc = sb("f_acc", [128, KCH * TT], F32, f(KCH * TT * 4))
    f_g = [sb(f"f_g{i}", [128, 16 * 256], BF16, f(16 * 256 * 2)) for i in range(2)]
    f_u = [sb(f"f_u{i}", [128, 16 * 256], BF16, f(16 * 256 * 2)) for i in range(2)]
    f_d = [sb(f"f_d{i}", [128, 16 * 256], BF16, f(16 * 256 * 2)) for i in range(2)]
    f_gg = [sb(f"f_gg{i}", [128, TT], F32, f(TT * 4)) for i in range(2)]
    f_x2p = [sb(f"f_x2p{i}", [128, TT], F32, f(TT * 4)) for i in range(2)]
    f_o = [sb(f"f_o{i}", [128, TT], F32, f(TT * 4)) for i in range(2)]
    f_top = c[0]

    c, f = mk_scr()
    p_w = [sb(f"p_w{i}", [128, 8192], F32, f(8192 * 4)) for i in range(2)]
    p_wb = [sb(f"p_wb{i}", [128, 8192], BF16, f(8192 * 2)) for i in range(2)]
    p_x1 = sb("p_x1", [128, ICH * 176], F32, f(ICH * 176 * 4))
    p_x2 = sb("p_x2", [128, DI], F32, f(DI * 4))
    p_top = c[0]

    total = SCR + max(m_top, s_top, q_top, o_top, f_top, p_top)
    assert total <= ARENA, f"SBUF overflow: {total}"

    _names = ["pe", "ve", "sc", "cc",
              "qc", "qp0", "qp1", "qx", "qsl0", "qsl1", "qy2",
              "qfg0", "qfg1", "qfu0", "qfu1", "qfd0", "qfd1",
              "qz0", "qz1", "qdl0", "qdl1", "qdu0", "qdu1",
              "qxc", "qbm", "qmu", "qbr", "qld0", "qld1", "qe",
              "qw80", "qw81", "qdq", "qy", "qx2", "qx2p0", "qx2p1", "qo0", "qo1",
              "qcs0", "qcs1", "qst", "qxp0", "qxp1", "gp"]
    sems = {k: Sem(es.enter_context(nc.semaphore(k))) for k in _names}
    pe, ve, scm, cc = (sems[k] for k in ("pe", "ve", "sc", "cc"))
    gpo = sems["gp"]
    qc, qx, qy2 = sems["qc"], sems["qx"], sems["qy2"]
    qp = [sems["qp0"], sems["qp1"]]
    qsl = [sems["qsl0"], sems["qsl1"]]
    qfg = [sems["qfg0"], sems["qfg1"]]
    qfu = [sems["qfu0"], sems["qfu1"]]
    qfd = [sems["qfd0"], sems["qfd1"]]
    qz = [sems["qz0"], sems["qz1"]]
    qdl = [sems["qdl0"], sems["qdl1"]]
    qdu = [sems["qdu0"], sems["qdu1"]]
    qxc, qbm, qmu, qbr, qe = sems["qxc"], sems["qbm"], sems["qmu"], sems["qbr"], sems["qe"]
    qld = [sems["qld0"], sems["qld1"]]
    qw8 = [sems["qw80"], sems["qw81"]]
    qdq, qy, qx2 = sems["qdq"], sems["qy"], sems["qx2"]
    qx2p = [sems["qx2p0"], sems["qx2p1"]]
    qo = [sems["qo0"], sems["qo1"]]
    qcs = [sems["qcs0"], sems["qcs1"]]
    qst = sems["qst"]
    qxp = [sems["qxp0"], sems["qxp1"]]

    B = Builder(nc)
    B.own = {"vector": sems["ve"], "scalar": sems["sc"], "gpsimd": sems["gp"]}
    em = B.emit
    guard = {}

    def gw(*keys):
        w = []
        for k in keys:
            w += guard.get(k, [])
        return w

    def gs(key, *ss):
        guard[key] = [(s, s.n) for s in ss]

    for (dsts, srct) in ((cw, convw), (dtbs, dtb), (dvs, dvec), (mskd, maskd),
                         (mskdc, maskdc), (mske, maske), (kcs, kcst)):
        em("sync", (), (lambda dsts, srct: lambda e: e.dma_start(out=dsts[:], in_=srct.ap()))(dsts, srct), qc, 16)
    mxf = m_x[:].bitcast(F32)
    em("sync", (), lambda e: e.dma_start(out=mxf[:, 0:128], in_=idm.ap()), qc, 16)
    em("sync", (), lambda e: e.dma_start(out=mxf[:, 128:129], in_=onescol.ap()), qc, 16)
    em("sync", (), lambda e: e.dma_start(out=mxf[:, 129:129 + UN], in_=alog.ap()), qc, 16)
    em("sync", (), lambda e: e.dma_start(
        out=p_x1[:].rearrange("p (a b) -> p a b", a=ICH),
        in_=wxt.ap().rearrange("(a p) b -> p a b", p=128)), qc, 16)
    em("sync", (), lambda e: e.dma_start(out=p_x2[:], in_=wdt.ap()), qc, 16)
    n_qc = qc.n
    em("vector", [(qc, n_qc)], lambda e: e.tensor_copy(idmb[:], mxf[:, 0:128]), ve, 1)
    em("vector", (), lambda e: e.tensor_copy(onesb[:], mxf[:, 128:129]), ve, 1)
    em("scalar", [(qc, n_qc)], lambda e: e.activation(apos[:], mxf[:, 129:129 + UN], AF.Exp), scm, 1)
    em("vector", [(scm, scm.n)], lambda e: e.tensor_scalar_mul(aneg[:], apos[:], -1.0), ve, 1)
    em("vector", (), lambda e: e.memset(tdp[:], 0.0), ve, 1)

    cast_guard = {0: ve.n, 1: ve.n}
    cast_sc0 = scm.n

    def cast_weight(src, dst, kchunks, mlen):
        msup = mlen // 512
        for kc in range(kchunks):
            b = kc % 2
            em("sync", [(ve, cast_guard[b]), (scm, cast_sc0)],
               (lambda kc, b, mlen, src: lambda e: e.dma_start(
                   out=p_w[b][:, 0:mlen],
                   in_=src.ap()[kc * 128:(kc + 1) * 128, :]))(kc, b, mlen, src), qp[b], 16)
            em("vector", [(qp[b], qp[b].n), (qcs[b], qcs[b].n)],
               (lambda b, mlen: lambda e: e.tensor_copy(
                   p_wb[b][:, 0:mlen], p_w[b][:, 0:mlen]))(b, mlen), ve, 1)
            cast_guard[b] = ve.n
            em("gpsimd", [(ve, ve.n)], (lambda kc, b, msup, kchunks, mlen, dst: lambda e: e.dma_start(
                out=rap(dst, kc * 128 * 512, (512, 128), (kchunks * 128 * 512, msup), (1, 512)),
                in_=p_wb[b][:, 0:mlen].rearrange("p (s m) -> p s m", m=512)))(kc, b, msup, kchunks, mlen, dst),
               qcs[b], 16)

    cast_weight(w1t, w1b, KCH, 2 * DI)
    cast_weight(wot, wob, ICH, DM)
    n_cs = (qcs[0].n, qcs[1].n)
    em("vector", [(qc, n_qc), (qcs[0], n_cs[0]), (qcs[1], n_cs[1])],
       lambda e: e.tensor_copy(wxtb[:], p_x1[:]), ve, 1)
    em("vector", (), lambda e: e.tensor_copy(wdtb[:], p_x2[:]), ve, 1)
    for _b in range(2):
        guard[("msl", _b)] = [(qcs[0], n_cs[0]), (qcs[1], n_cs[1]), (ve, ve.n)]
        guard[("osl", _b)] = [(qcs[0], n_cs[0]), (qcs[1], n_cs[1]), (ve, ve.n)]

    def ln_stats_apply(xbf, xsq, xlnout, ncols, in_waits, bc1, bc2, wr_waits=()):
        em("scalar", list(in_waits) + list(wr_waits), lambda e: e.activation(
            xsq[:, 0:KCH * ncols], xbf[:, 0:KCH * ncols], AF.Square), scm, 1)
        vready, sready = ve.n, scm.n

        def mm1(e):
            for kc in range(KCH):
                ins = nc.tensor.matmul(psum[4][0:1, 0:ncols], onesb[:, 0:1],
                                       xbf[:, kc * ncols:(kc + 1) * ncols],
                                       start=(kc == 0), stop=(kc == KCH - 1))
            return ins

        def mm2(e):
            for kc in range(KCH):
                ins = nc.tensor.matmul(psum[5][0:1, 0:ncols], onesb[:, 0:1],
                                       xsq[:, kc * ncols:(kc + 1) * ncols],
                                       start=(kc == 0), stop=(kc == KCH - 1))
            return ins
        em("tensor", list(in_waits) + gw("pb4"), mm1, pe, 1)
        em("tensor", [(scm, sready)] + gw("pb5"), mm2, pe, 1)
        pdone = pe.n
        em("vector", [(pe, pdone)], lambda e: e.tensor_scalar_mul(stat[0:1, 0:ncols], psum[4][0:1, 0:ncols], 1.0 / DM), ve, 1)
        gs("pb4", ve)
        em("vector", (), lambda e: e.tensor_scalar_mul(stat[0:1, 512:512 + ncols], psum[5][0:1, 0:ncols], 1.0 / DM), ve, 1)
        gs("pb5", ve)
        em("vector", (), lambda e: e.tensor_mul(stat[0:1, 1024:1024 + ncols], stat[0:1, 0:ncols], stat[0:1, 0:ncols]), ve, 1)
        em("vector", (), lambda e: e.tensor_sub(stat[0:1, 512:512 + ncols], stat[0:1, 512:512 + ncols], stat[0:1, 1024:1024 + ncols]), ve, 1)
        em("scalar", [(ve, ve.n)], lambda e: e.activation(stat[0:1, 1024:1024 + ncols], stat[0:1, 512:512 + ncols], AF.Ln, bias=kcs[0:1, 0:1]), scm, 1)
        em("scalar", (), lambda e: e.activation(stat[0:1, 512:512 + ncols], stat[0:1, 1024:1024 + ncols], AF.Exp, scale=kcs[0:1, 1:2]), scm, 1)
        em("vector", [(scm, scm.n)], lambda e: e.tensor_mul(stat[0:1, 1024:1024 + ncols], stat[0:1, 0:ncols], stat[0:1, 512:512 + ncols]), ve, 1)
        vfin = ve.n
        ln_mid = (pdone, vready if False else ve.n, scm.n)
        em("gpsimd", [(ve, vfin), (qmu, qmu.n)], lambda e: e.dma_start(out=rap(muv, 0, (512, 1), (1, ncols)), in_=stat[0:1, 512:512 + ncols]), qmu, 16)
        em("gpsimd", (), lambda e: e.dma_start(out=rap(muv, 512, (512, 1), (1, ncols)), in_=stat[0:1, 1024:1024 + ncols]), qmu, 16)
        gn = qmu.n
        em("gpsimd", [(qmu, gn)], (lambda bc1: lambda e: e.dma_start(out=bc1[:, 0:ncols], in_=rap(muv, 0, (0, 128), (1, ncols))))(bc1), qmu, 16)
        em("gpsimd", (), (lambda bc2: lambda e: e.dma_start(out=bc2[:, 0:ncols], in_=rap(muv, 512, (0, 128), (1, ncols))))(bc2), qmu, 16)
        gn2 = qmu.n
        for kc in range(KCH):
            em("vector", [(qmu, gn2)], (lambda kc: lambda e: e.tensor_mul(
                xlnout[:, kc * ncols:(kc + 1) * ncols],
                xbf[:, kc * ncols:(kc + 1) * ncols], bc1[:, 0:ncols]))(kc), ve, 1)
            em("vector", (), (lambda kc: lambda e: e.tensor_sub(
                xlnout[:, kc * ncols:(kc + 1) * ncols],
                xlnout[:, kc * ncols:(kc + 1) * ncols], bc2[:, 0:ncols]))(kc), ve, 1)
        gs("obc", ve)

    mx_guard = [(qcs[0], n_cs[0]), (qcs[1], n_cs[1]), (ve, ve.n), (scm, scm.n)]
    inproj_guard = [[]]
    for j in range(5):
        is_halo = (j == 0)
        xw = list(mx_guard) + [(qx, qx.n)]
        em("gpsimd", xw, (lambda j: lambda e: e.dma_start(
            out=m_x[:, 0:KCH * TM].rearrange("p (c t) -> p c t", c=KCH),
            in_=xin.ap().rearrange("(c p) t -> p c t", p=128)[:, :, j * TM:(j + 1) * TM]))(j), qx, 16)
        ln_stats_apply(m_x, m_xln, m_xln, TM, [(qx, qx.n)], m_bc1, m_bc2,
                       wr_waits=inproj_guard[0])
        xln_v = ve.n
        mx_guard = [(ve, ve.n), (scm, scm.n), (pe, pe.n)]

        mlim = 16 if not is_halo else 8
        for ms in range(mlim):
            sb_i = ms % 2
            em("sync", gw(("msl", sb_i)), (lambda ms, sb_i: lambda e: e.dma_start(
                out=m_slab[sb_i][:].rearrange("p (a b) -> p a b", b=512),
                in_=rap(w1b, ms * KCH * 128 * 512, (512, 128), (128 * 512, KCH), (1, 512))))(ms, sb_i),
               qsl[sb_i], 16)
            slab_w = (qsl[sb_i], qsl[sb_i].n)
            for mc in range(4):
                m = ms * 4 + mc
                bank = mc % 2

                def mmgrp(mc, bank, sb_i):
                    def go(e):
                        for kc in range(KCH):
                            ins = nc.tensor.matmul(
                                psum[bank][:, 0:TM],
                                m_slab[sb_i][:, kc * 512 + mc * 128:kc * 512 + mc * 128 + 128],
                                m_xln[:, kc * TM:(kc + 1) * TM],
                                start=(kc == 0), stop=(kc == KCH - 1))
                        return ins
                    return go
                em("tensor", [slab_w, (ve, xln_v)] + gw(("pb", bank)), mmgrp(mc, bank, sb_i), pe, 1)
                mm_n = pe.n
                if mc == 3:
                    gs(("msl", sb_i), pe)
                if ms == mlim - 1 and mc == 3:
                    inproj_guard[0] = [(pe, pe.n)]
                if m < ICH:
                    em("vector", [(pe, mm_n)], (lambda m, bank: lambda e: e.tensor_copy(
                        m_xpre[:, m * (TM + 3) + 3:m * (TM + 3) + 3 + TM], psum[bank][:, 0:TM]))(m, bank), ve, 1)
                    gs(("pb", bank), ve)
                else:
                    zch = m - ICH
                    bq = zch % 2
                    em("scalar", [(pe, mm_n)] + gw(("msz", bq)), (lambda bank, bq: lambda e: e.activation(
                        m_sz[bq][:], psum[bank][:, 0:TM], AF.Silu))(bank, bq), scm, 1)
                    gs(("pb", bank), scm)
                    em("gpsimd", [(scm, scm.n)], (lambda zch, j, bq: lambda e: e.dma_start(
                        out=rap(sz_sp, zch * 128 * T + (j - 1) * TM, (T, 128), (1, TM)),
                        in_=m_sz[bq][:]))(zch, j, bq), qz[bq], 16)
                    gs(("msz", bq), sems["qz%d" % bq])
        evac_v = ve.n

        if is_halo:
            def tailcopy0(e):
                for chx in range(ICH):
                    ins = e.tensor_copy(m_xpre[:, chx * (TM + 3):chx * (TM + 3) + 3],
                                        m_xpre[:, chx * (TM + 3) + TM:chx * (TM + 3) + TM + 3])
                return ins
            em("vector", [(ve, evac_v)], tailcopy0, ve, 1)
            continue

        for chx in range(ICH):
            b = chx % 2
            base = chx * (TM + 3)

            em("vector", [(scm, scm.n)], (lambda chx, b, base: lambda e: e.tensor_scalar(
                m_cv[b][:], m_xpre[:, base:base + TM],
                cw[:, chx * DC:chx * DC + 1], None, ALU.mult))(chx, b, base), ve, 1)
            for w in range(1, DC):
                em("vector", (), (lambda chx, b, base, w: lambda e: e.scalar_tensor_tensor(
                    m_cv[b][:], m_xpre[:, base + w:base + w + TM],
                    cw[:, chx * DC + w:chx * DC + w + 1], m_cv[b][:],
                    ALU.mult, ALU.add))(chx, b, base, w), ve, 1)
            em("scalar", [(ve, ve.n)] + (gw("mxc") if chx == 0 else []), (lambda chx, b: lambda e: e.activation(
                m_xc[:, chx * TM:(chx + 1) * TM], m_cv[b][:], AF.Silu))(chx, b), scm, 1)
        conv_s = scm.n

        def tailcopy(e):
            for chx in range(ICH):
                ins = e.tensor_copy(m_xpre[:, chx * (TM + 3):chx * (TM + 3) + 3],
                                    m_xpre[:, chx * (TM + 3) + TM:chx * (TM + 3) + TM + 3])
            return ins
        em("vector", [(scm, conv_s)], tailcopy, ve, 1)
        em("gpsimd", [(scm, conv_s)], (lambda j: lambda e: e.dma_start(
            out=rap(xc_sp, (j - 1) * TM, (T, 128), (128 * T, ICH), (1, TM)),
            in_=m_xc[:].rearrange("p (c t) -> p c t", c=ICH)))(j), qxc, 16)
        gs("mxc", qxc)

        def xp_dt(e):
            for kc in range(ICH):
                ins = nc.tensor.matmul(psum[6][:, 0:TM], wxtb[:, kc * 176:kc * 176 + 128],
                                       m_xc[:, kc * TM:(kc + 1) * TM],
                                       start=(kc == 0), stop=(kc == ICH - 1))
            return ins

        def xp_bc(e):
            for kc in range(ICH):
                ins = nc.tensor.matmul(psum[7][0:48, 0:TM], wxtb[:, kc * 176 + 128:kc * 176 + 176],
                                       m_xc[:, kc * TM:(kc + 1) * TM],
                                       start=(kc == 0), stop=(kc == ICH - 1))
            return ins
        em("tensor", [(scm, conv_s)] + gw("pb6"), xp_dt, pe, 1)
        em("tensor", gw("pb7"), xp_bc, pe, 1)
        xp_n = pe.n
        em("vector", [(pe, xp_n)], lambda e: e.tensor_copy(m_dt[:], psum[6][:, 0:TM]), ve, 1)
        gs("pb6", ve)
        em("vector", [(pe, xp_n), (qbm, qbm.n)], lambda e: e.tensor_copy(m_bc[0:48, 0:TM], psum[7][0:48, 0:TM]), ve, 1)
        gs("pb7", ve)
        em("gpsimd", [(ve, ve.n)], (lambda j: lambda e: e.dma_start(
            out=rap(bmcm, (j - 1) * TM, (T, 48), (1, TM)), in_=m_bc[0:48, 0:TM]))(j), qbm, 16)

        dt_v = ve.n
        for chx in range(ICH):
            bank = 2 + chx % 2
            bq = chx % 2
            em("tensor", [(ve, dt_v)] + gw(("pb", bank)), (lambda chx, bank: lambda e: nc.tensor.matmul(
                psum[bank][:, 0:TM], wdtb[:, chx * 128:(chx + 1) * 128], m_dt[:],
                start=True, stop=True))(chx, bank), pe, 1)
            mm_n = pe.n
            em("scalar", [(pe, mm_n)] + gw("msp"), (lambda chx, bank: lambda e: e.activation(
                m_sp1[:], psum[bank][:, 0:TM], AF.Relu, bias=dtbs[:, chx:chx + 1]))(chx, bank), scm, 1)
            em("scalar", (), (lambda chx, bank: lambda e: e.activation(
                m_sp2[:], psum[bank][:, 0:TM], AF.Abs, bias=dtbs[:, chx:chx + 1]))(chx, bank), scm, 1)
            gs(("pb", bank), scm)
            em("scalar", (), lambda e: e.activation(m_sp2[:], m_sp2[:], AF.Exp, scale=kcs[:, 2:3]), scm, 1)
            em("scalar", (), lambda e: e.activation(m_sp2[:], m_sp2[:], AF.Ln, bias=kcs[:, 3:4]), scm, 1)
            sp_s = scm.n
            em("vector", [(scm, sp_s), (qdl[bq], qdl[bq].n)], (lambda chx, bq, j: lambda e: e.scalar_tensor_tensor(
                m_dl[bq][:], m_sp1[:], 1.0, m_sp2[:], ALU.mult, ALU.add,
                accum_out=tdp[:, chx * 4 + (j - 1):chx * 4 + j]))(chx, bq, j), ve, 1)
            gs("msp", ve)
            em("vector", [(qdu[bq], qdu[bq].n)], (lambda chx, bq: lambda e: e.tensor_mul(
                m_du[bq][:], m_dl[bq][:], m_xc[:, chx * TM:(chx + 1) * TM]))(chx, bq), ve, 1)
            dl_v = ve.n
            em("gpsimd", [(ve, dl_v)], (lambda chx, bq, j: lambda e: e.dma_start(
                out=rap(dl_sp, chx * 128 * T + (j - 1) * TM, (T, 128), (1, TM)),
                in_=m_dl[bq][:]))(chx, bq, j), qdl[bq], 16)
            em("gpsimd", (), (lambda chx, bq, j: lambda e: e.dma_start(
                out=rap(du_sp, chx * 128 * T + (j - 1) * TM, (T, 128), (1, TM)),
                in_=m_du[bq][:]))(chx, bq, j), qdu[bq], 16)

    m_v = ve.n
    spill_done = [(qz[0], qz[0].n), (qz[1], qz[1].n), (qdl[0], qdl[0].n), (qdl[1], qdl[1].n),
                  (qdu[0], qdu[0].n), (qdu[1], qdu[1].n), (qxc, qxc.n), (qbm, qbm.n)]

    em("vector", [(ve, m_v)], lambda e: e.tensor_add(
        td[:], tdp[:].rearrange("p (c f) -> p c f", f=4)[:, :, 0],
        tdp[:].rearrange("p (c f) -> p c f", f=4)[:, :, 1]), ve, 1)
    em("vector", (), lambda e: e.tensor_add(
        stat[:, 0:ICH], tdp[:].rearrange("p (c f) -> p c f", f=4)[:, :, 2],
        tdp[:].rearrange("p (c f) -> p c f", f=4)[:, :, 3]), ve, 1)
    em("vector", (), lambda e: e.tensor_add(td[:], td[:], stat[:, 0:ICH]), ve, 1)
    td_v = ve.n

    def dc_ops(e):
        for chx in range(ICH):
            ins = e.activation(dcal[:, chx * DN:(chx + 1) * DN], aneg[:, chx * DN:(chx + 1) * DN],
                               AF.Exp, scale=td[:, chx:chx + 1])
        return ins
    em("scalar", [(ve, td_v)], dc_ops, scm, 1)
    dc_s = scm.n

    first_pass_load = [True]

    def build_reps(half, with_c, bw=()):
        for n in range(DN):
            em("gpsimd", list(bw) if n == 0 else (), (lambda n, half: lambda e: e.dma_start(
                out=s_brep[:, n * HALF:(n + 1) * HALF],
                in_=rap(bmcm, n * T + half * HALF, (0, 128), (1, HALF))))(n, half), qbr, 16)
            if with_c:
                em("gpsimd", (), (lambda n, half: lambda e: e.dma_start(
                    out=s_crep[:, n * HALF:(n + 1) * HALF],
                    in_=rap(bmcm, (32 + n) * T + half * HALF, (0, 128), (1, HALF))))(n, half), qbr, 16)
        return qbr.n

    POOL_N = ()

    def scan_pass(pass2, pre_reps=None, hook=None):
        for half in range(2):
            bw = [(ve, ve.n), (scm, scm.n), (gpo, gpo.n)]
            sync_gate = []
            if first_pass_load[0]:
                bw = bw + spill_done
                sync_gate = bw
                first_pass_load[0] = False
            if pass2 and half == 0 and pre_reps is not None:
                reps_g = pre_reps
            else:
                reps_g = build_reps(half, with_c=pass2, bw=bw)
            for chx in range(ICH):
                b = chx % 2
                em("sync", gw(("sld", b)) + (sync_gate if chx < 2 else []), (lambda chx, b, half: lambda e: e.dma_start(
                    out=s_dl[b][:], in_=rap(dl_sp, chx * 128 * T + half * HALF, (T, 128), (1, HALF))))(chx, b, half), qld[b], 16)
                em("sync", (), (lambda chx, b, half: lambda e: e.dma_start(
                    out=s_du[b][:], in_=rap(du_sp, chx * 128 * T + half * HALF, (T, 128), (1, HALF))))(chx, b, half), qld[b], 16)
                if pass2:
                    em("sync", (), (lambda chx, b, half: lambda e: e.dma_start(
                        out=s_xc[b][:], in_=rap(xc_sp, chx * 128 * T + half * HALF, (T, 128), (1, HALF))))(chx, b, half), qld[b], 16)
                    em("sync", (), (lambda chx, b, half: lambda e: e.dma_start(
                        out=s_sz[b][:], in_=rap(sz_sp, chx * 128 * T + half * HALF, (T, 128), (1, HALF))))(chx, b, half), qld[b], 16)
                ld_g = (qld[b], qld[b].n)
                if not pass2:
                    em("vector", [ld_g, (scm, scm.n)], (lambda b: lambda e: e.tensor_tensor_scan(
                        s_cd[:], s_one[:], s_dl[b][:], 0.0, ALU.mult, ALU.add))(b), ve, 1)
                    em("vector", (), (lambda chx: lambda e: e.tensor_scalar(
                        s_bx[:, 0:DN], aneg[:, chx * DN:(chx + 1) * DN],
                        s_cd[:, HALF - 1:HALF], None, ALU.mult))(chx), ve, 1)
                    cd_v = ve.n
                    em("scalar", [(ve, cd_v)], lambda e: e.activation(
                        s_bx[:, 16:16 + DN], s_bx[:, 0:DN], AF.Exp), scm, 1)
                    dch_s = scm.n
                for n in range(DN):
                    unit = chx * DN + n
                    ab = n % 2
                    pool = n in POOL_N
                    ceng = "gpsimd" if pool else "vector"
                    csem = gpo if pool else ve
                    t_u = (p_u if pool else s_u)[ab]
                    t_t = (p_t if pool else s_tmp)[ab]
                    if not pass2:
                        em("scalar", [(ve, cd_v)] + gw(("sa", ab)), (lambda chx, n, ab: lambda e: e.activation(
                            s_a[ab][:], s_cd[:], AF.Exp,
                            scale=apos[:, chx * DN + n:chx * DN + n + 1],
                            bias=s_bx[:, n:n + 1]))(chx, n, ab), scm, 1)
                        a_s = scm.n
                        em("vector", [(qbr, reps_g), ld_g], (lambda n, b, ab: lambda e: e.tensor_mul(
                            s_u[ab][:], s_du[b][:], s_brep[:, n * HALF:(n + 1) * HALF]))(n, b, ab), ve, 1)
                        em("vector", [(scm, a_s)], (lambda ab, unit, half: lambda e: e.scalar_tensor_tensor(
                            s_tmp[ab][:], s_a[ab][:], 1.0, s_u[ab][:], ALU.mult, ALU.mult,
                            accum_out=(e0a if half == 0 else e0b)[:, unit:unit + 1]))(ab, unit, half), ve, 1)
                        gs(("sa", ab), ve)
                        continue
                    em("scalar", [ld_g] + gw(("sa", ab)), (lambda chx, n, b, ab: lambda e: e.activation(
                        s_a[ab][:], s_dl[b][:], AF.Exp,
                        scale=aneg[:, chx * DN + n:chx * DN + n + 1]))(chx, n, b, ab), scm, 1)
                    a_s = scm.n
                    em(ceng, [(qbr, reps_g), ld_g] + gw(("su", pool, ab)), (lambda n, b, t_u: lambda e: e.tensor_mul(
                        t_u[:], s_du[b][:], s_brep[:, n * HALF:(n + 1) * HALF]))(n, b, t_u),
                       None if pool else ve, 1)
                    u_c = csem.n
                    uw = [(gpo, u_c)] if pool else []
                    if True:
                        em("vector", [(scm, a_s)] + uw + gw(("sh", ab)), (lambda ab, t_u, unit, half: lambda e: e.tensor_tensor_scan(
                            s_h[ab][:], s_a[ab][:], t_u[:],
                            (hin[:, unit:unit + 1] if half == 0 else hcar[:, unit:unit + 1]),
                            ALU.mult, ALU.add))(ab, t_u, unit, half), ve, 1)
                        gs(("sa", ab), ve)
                        gs(("su", pool, ab), ve)
                        scan_v = ve.n
                        if half == 0:
                            em("scalar", [(ve, scan_v)], (lambda ab, unit: lambda e: e.copy(
                                hcar[:, unit:unit + 1], s_h[ab][:, HALF - 1:HALF]))(ab, unit), scm, 1)
                            gs(("sh", ab), scm)
                        em(ceng, ([(ve, scan_v)] if pool else []) + gw(("stmp", pool, ab)),
                           (lambda n, ab, t_t: lambda e: e.tensor_mul(
                               t_t[:], s_h[ab][:], s_crep[:, n * HALF:(n + 1) * HALF]))(n, ab, t_t),
                           None if pool else ve, 1)
                        tmp_c = csem.n

                        def ymm(n, t_t):
                            def go(e):
                                for s_i in range(2):
                                    ins = nc.tensor.matmul(psum[s_i][:, 0:TT], idmb[:],
                                                           t_t[:, s_i * TT:(s_i + 1) * TT],
                                                           start=(n == 0), stop=(n == DN - 1))
                                return ins
                            return go
                        em("tensor", [(csem, tmp_c)] + (gw("pby") if n == 0 else []), ymm(n, t_t), pe, 1)
                        gs(("stmp", pool, ab), pe)
                if not pass2:
                    if half == 1:
                        em("vector", [(scm, dch_s)], (lambda chx: lambda e: e.tensor_mul(
                            s_e[0][:, 0:DN], e0a[:, chx * DN:(chx + 1) * DN],
                            s_bx[:, 16:16 + DN]))(chx), ve, 1)
                        em("vector", (), (lambda chx: lambda e: e.tensor_add(
                            e0b[:, chx * DN:(chx + 1) * DN], e0b[:, chx * DN:(chx + 1) * DN],
                            s_e[0][:, 0:DN]))(chx), ve, 1)
                    gs(("sld", b), scm, ve, gpo)
                if pass2:
                    y_p = pe.n
                    for s_i in range(2):
                        em("vector", [(pe, y_p), (qy, qy.n)], (lambda chx, b, s_i: lambda e: e.scalar_tensor_tensor(
                            s_y1[:], s_xc[b][:, s_i * TT:(s_i + 1) * TT], dvs[:, chx:chx + 1],
                            psum[s_i][:, 0:TT], ALU.mult, ALU.add))(chx, b, s_i), ve, 1)
                        em("vector", (), (lambda b, s_i: lambda e: e.tensor_mul(
                            s_y2[:], s_y1[:], s_sz[b][:, s_i * TT:(s_i + 1) * TT]))(b, s_i), ve, 1)
                        em("sync", [(ve, ve.n)], (lambda chx, half, s_i: lambda e: e.dma_start(
                            out=rap(y2_sp, chx * 128 * T + half * HALF + s_i * TT, (T, 128), (1, TT)),
                            in_=s_y2[:]))(chx, half, s_i), qy, 16)
                    gs("pby", ve)
                    gs(("sld", b), scm, ve, gpo)
                if hook is not None:
                    hook()

    def finish():
        if debug and stage >= 2:
            em("gpsimd", [(cc, cc.n), (ve, ve.n), (scm, scm.n), (pe, pe.n)],
               lambda e: e.dma_start(out=e0dbg.ap(), in_=e0loc.ap()), sems["qe"], 16)
            if stage >= 3:
                em("gpsimd", (), lambda e: e.dma_start(out=wldbg.ap(), in_=wloc.ap()), sems["qe"], 16)
        allw = [(s, s.n) for s in sems.values() if s.n > 0]
        em("gpsimd", allw, None)
        B.run()
        es.close()
        return nc

    def dequant_unit(pk, sct, kind, wt):
        qb = wt % 2
        em("gpsimd", gw(("qpk", kind, qb)), (lambda pk, wt, qb: lambda e: e.dma_start(
            out=q_pk[qb][:],
            in_=pk.ap().rearrange("(w p) c -> w p c", p=128)[wt]))(pk, wt, qb), qdq, 16)
        pk_g = qdq.n
        for kq in range(8):
            em("gpsimd", gw(("qst", kind, qb)) if kq == 0 else (), (lambda sct, wt, kq, qb: lambda e: e.dma_start(
                out=q_st[qb][16 * kq:16 * (kq + 1), :],
                in_=rap(sct, (wt * 8 + kq) * 2048, (0, 16), (1, 2048))))(sct, wt, kq, qb), qst, 16)
        st_g = qst.n
        for hi in range(2):
            wb = hi
            if hi:
                em("vector", [(qdq, pk_g)], (lambda qb: lambda e: e.tensor_scalar(
                    q_u8[:], q_pk[qb][:], 4, 15,
                    ALU.logical_shift_right, ALU.bitwise_and))(qb), ve, 1)
            else:
                em("vector", [(qdq, pk_g)], (lambda qb: lambda e: e.tensor_single_scalar(
                    q_u8[:], q_pk[qb][:], 15, ALU.bitwise_and))(qb), ve, 1)
            if hi:
                gs(("qpk", kind, qb), ve)
            em("vector", (), lambda e: e.tensor_copy(q_vf[:], q_u8[:]), ve, 1)
            em("vector", (), lambda e: e.tensor_single_scalar(q_b3[:], q_vf[:], 8.0, ALU.is_ge), ve, 1)
            em("vector", (), lambda e: e.scalar_tensor_tensor(q_vf[:], q_b3[:], -8.0, q_vf[:], ALU.mult, ALU.add), ve, 1)
            m_v2 = ve.n
            em("scalar", [(ve, m_v2)], lambda e: e.activation(q_r1[:], q_vf[:], AF.Relu, bias=kcs[:, 4:5]), scm, 1)
            em("scalar", (), lambda e: e.activation(q_r2[:], q_vf[:], AF.Relu, bias=kcs[:, 5:6]), scm, 1)
            r_s = scm.n
            em("vector", [(scm, r_s)], lambda e: e.tensor_add(q_r1[:], q_vf[:], q_r1[:]), ve, 1)
            em("vector", (), lambda e: e.scalar_tensor_tensor(q_r1[:], q_r1[:], 0.5, q_r2[:], ALU.mult, ALU.add), ve, 1)
            em("vector", (), lambda e: e.tensor_scalar(q_b3[:], q_b3[:], -2.0, 1.0, ALU.mult, ALU.add), ve, 1)
            em("vector", [(qst, st_g)], (lambda qb: lambda e: e.tensor_mul(q_b3[:], q_b3[:], q_st[qb][:]))(qb), ve, 1)
            if hi:
                gs(("qst", kind, qb), ve)
            em("vector", gw(("qw", wb)), (lambda wb: lambda e: e.tensor_mul(
                q_w[wb][:], q_r1[:], q_b3[:]))(wb), ve, 1)
            w_v = ve.n
            if kind < 2:
                kc = wt + 8 * hi
                em("gpsimd", [(ve, w_v)], (lambda kind, kc, wb: lambda e: e.dma_start(
                    out=rap(wloc, kind * 2048 * 2048 + kc * 128 * 256,
                            (256, 128), (16 * 128 * 256, 8), (1, 256)),
                    in_=q_w[wb][:].rearrange("p (a c) -> p a c", a=8)))(kind, kc, wb), qw8[wb], 16)
            else:
                nr = 128 * wt if hi == 0 else 1024 + 128 * wt
                em("gpsimd", [(ve, w_v)], (lambda nr, wb: lambda e: e.dma_start(
                    out=rap(wloc, 2 * 2048 * 2048 + nr * 256,
                            (256, 128), (2048 * 256, 8), (1, 256)),
                    in_=q_w[wb][:].rearrange("p (a c) -> p a c", a=8)))(nr, wb), qw8[wb], 16)
            gs(("qw", wb), sems["qw8%d" % wb])

    dq_units = ([(gpk, gsc, 0, wt) for wt in range(8)] +
                [(upk, usc, 1, wt) for wt in range(8)] +
                [(dpk, dsc, 2, wt) for wt in range(8)])
    dq_i = [0]

    def dq_hook():
        if dq_i[0] < len(dq_units):
            dequant_unit(*dq_units[dq_i[0]])
            dq_i[0] += 1

    if stage <= 1:
        return finish()

    em("vector", (), lambda e: e.memset(s_one[:], 1.0), ve, 1)
    scan_pass(pass2=False, hook=(dq_hook if stage >= 3 else None))
    p1_v = ve.n

    em("gpsimd", [(ve, p1_v), (scm, dc_s), (scm, scm.n), (gpo, gpo.n)], lambda e: e.dma_start(
        out=rap(e0loc, 0, (1024, 128), (1, 512)), in_=e0b[:]), qe, 16)
    em("gpsimd", (), lambda e: e.dma_start(
        out=rap(e0loc, 512, (1024, 128), (1, 512)), in_=dcal[:]), qe, 16)
    em("gpsimd", [(qe, qe.n)], lambda e: e.collective_compute(
        "AllGather", ALU.bypass, replica_groups=[list(range(NCORE))],
        ins=[e0loc.ap().opt()], outs=[e0g.ap().opt()]), cc, 1)
    cc_e0 = cc.n
    em("vector", [(cc, cc_e0)], lambda e: e.memset(hin[:], 0.0), ve, 1)
    for r in range(7):
        em("gpsimd", [(cc, cc_e0), (ve, ve.n)], (lambda r: lambda e: e.dma_start(
            out=s_e[0][:], in_=rap(e0g, r * 128 * 1024 + 0, (1024, 128), (1, 512))))(r), qe, 16)
        em("gpsimd", (), (lambda r: lambda e: e.dma_start(
            out=s_e[1][:], in_=rap(e0g, r * 128 * 1024 + 512, (1024, 128), (1, 512))))(r), qe, 16)
        rg = qe.n
        em("vector", [(qe, rg)], (lambda r: lambda e: e.tensor_scalar(
            s_e[1][:], s_e[1][:], mskd[:, r:r + 1], mskdc[:, r:r + 1], ALU.mult, ALU.add))(r), ve, 1)
        em("vector", (), lambda e: e.tensor_mul(hin[:], hin[:], s_e[1][:]), ve, 1)
        em("vector", (), (lambda r: lambda e: e.tensor_scalar(
            s_e[0][:], s_e[0][:], mske[:, r:r + 1], None, ALU.mult))(r), ve, 1)
        em("vector", (), lambda e: e.tensor_add(hin[:], hin[:], s_e[0][:]), ve, 1)

    pre_reps2 = build_reps(0, with_c=True, bw=[(ve, p1_v), (gpo, gpo.n)])
    if stage >= 3:
        while dq_i[0] < len(dq_units):
            dq_hook()
        em("gpsimd", [(qw8[0], qw8[0].n), (qw8[1], qw8[1].n)], lambda e: e.collective_compute(
            "AllGather", ALU.bypass, replica_groups=[list(range(NCORE))],
            ins=[wloc.ap().opt()], outs=[wgth.ap().opt()]), cc, 1)
    cc_w = cc.n

    if stage <= 2:
        return finish()

    if stage <= 3:
        return finish()

    scan_pass(pass2=True, pre_reps=pre_reps2)
    p2_y = qy.n

    if stage <= 4:
        return finish()

    first_ffn_load = [True]
    for t in range(4):
        em("sync", [(qy, p2_y), (pe, pe.n), (qy2, qy2.n), (qo[0], qo[0].n), (qo[1], qo[1].n),
                    (qx2p[0], qx2p[0].n)] + gw("oy2"), (lambda t: lambda e: e.dma_start(
            out=o_y2[:].rearrange("p (c x) -> p c x", c=ICH),
            in_=rap(y2_sp, t * TT, (T, 128), (128 * T, ICH), (1, TT))))(t), qy2, 16)
        y2_w = (qy2, qy2.n)
        for ms in range(4):
            sb_i = ms % 2
            em("sync", gw(("osl", sb_i)), (lambda ms, sb_i: lambda e: e.dma_start(
                out=o_slab[sb_i][:].rearrange("p (a b) -> p a b", b=512),
                in_=rap(wob, ms * ICH * 128 * 512, (512, 128), (128 * 512, ICH), (1, 512))))(ms, sb_i),
               qsl[sb_i], 16)
            slab_w = (qsl[sb_i], qsl[sb_i].n)
            for mc in range(4):
                m = ms * 4 + mc
                bank = mc % 2
                xp_s = m % 2
                em("sync", gw(("oxp", xp_s)), (lambda t, m, xp_s: lambda e: e.dma_start(
                    out=o_xp[xp_s][:],
                    in_=xin.ap()[:, TM + t * TT:TM + (t + 1) * TT]
                        .rearrange("(c p) x -> c p x", p=128)[m]))(t, m, xp_s), qxp[xp_s], 16)
                xp_w = (qxp[xp_s], qxp[xp_s].n)

                def opmm(mc, bank, sb_i):
                    def go(e):
                        for kc in range(ICH):
                            ins = nc.tensor.matmul(
                                psum[bank][:, 0:TT],
                                o_slab[sb_i][:, kc * 512 + mc * 128:kc * 512 + mc * 128 + 128],
                                o_y2[:, kc * TT:(kc + 1) * TT],
                                start=(kc == 0), stop=(kc == ICH - 1))
                        return ins
                    return go
                em("tensor", [slab_w, y2_w] + gw(("pb", bank)), opmm(mc, bank, sb_i), pe, 1)
                if mc == 3:
                    gs(("osl", sb_i), pe)
                em("vector", [(pe, pe.n), xp_w] + gw("ox2"), (lambda m, bank, xp_s: lambda e: e.tensor_add(
                    o_x2[:, m * TT:(m + 1) * TT], o_xp[xp_s][:],
                    psum[bank][:, 0:TT]))(m, bank, xp_s), ve, 1)
                gs(("pb", bank), ve)
                gs(("oxp", xp_s), ve)
        x2_v = ve.n
        gs("oy2", pe)
        em("gpsimd", [(ve, x2_v), (qx2, 16 * t)], (lambda t: lambda e: e.dma_start(
            out=rap(x2_sp, t * TT, (T, 128), (128 * T, KCH), (1, TT)),
            in_=o_x2[:].rearrange("p (c x) -> p c x", c=KCH)))(t), qx2, 16)
        gs("ox2", qx2)
        em("vector", [(ve, x2_v)] + gw("x2ln"), lambda e: e.tensor_copy(o_x2ln[:], o_x2[:]), ve, 1)
        ln_stats_apply(o_x2ln, o_xsq, o_x2ln, TT, [(ve, ve.n)], o_bc1, o_bc2)
        x2ln_v = ve.n

        for _b in range(2):
            for _k in ("fg", "fu", "fd"):
                guard[(_k, _b)] = guard.get((_k, _b), []) + [(pe, pe.n)]
        dbg_gate = []
        for ih in range(2):
            for r in range(4 * ih, 4 * ih + 4):
                for ncc in range(8):
                    nb = (r * 8 + ncc) % 2
                    lw = gw(("fg", nb)) + dbg_gate
                    if first_ffn_load[0]:
                        lw = lw + [(cc, cc_w)]
                        first_ffn_load[0] = False
                    em("sync", lw, (lambda r, ncc, nb: lambda e: e.dma_start(
                        out=f_g[nb][:].rearrange("p (a c) -> p a c", a=16),
                        in_=rap(wgth, r * 6144 * 2048 + ncc * 16 * 128 * 256,
                                (256, 128), (128 * 256, 16), (1, 256))))(r, ncc, nb), qfg[nb], 16)
                    em("sync", gw(("fu", nb)), (lambda r, ncc, nb: lambda e: e.dma_start(
                        out=f_u[nb][:].rearrange("p (a c) -> p a c", a=16),
                        in_=rap(wgth, r * 6144 * 2048 + 2048 * 2048 + ncc * 16 * 128 * 256,
                                (256, 128), (128 * 256, 16), (1, 256))))(r, ncc, nb), qfu[nb], 16)
                    gq = [(qfg[nb], qfg[nb].n), (qfu[nb], qfu[nb].n)]
                    for sub in range(2):
                        lnch = (r - 4 * ih) * 16 + ncc * 2 + sub
                        gbank = 4 + sub
                        ubank = 6 + sub

                        def gumm(nb, sub, gbank, ubank):
                            def go(e):
                                for kc in range(KCH):
                                    nc.tensor.matmul(psum[gbank][:, 0:TT],
                                                     f_g[nb][:, kc * 256 + sub * 128:kc * 256 + sub * 128 + 128],
                                                     o_x2ln[:, kc * TT:(kc + 1) * TT],
                                                     start=(kc == 0), stop=(kc == KCH - 1))
                                for kc in range(KCH):
                                    ins = nc.tensor.matmul(psum[ubank][:, 0:TT],
                                                           f_u[nb][:, kc * 256 + sub * 128:kc * 256 + sub * 128 + 128],
                                                           o_x2ln[:, kc * TT:(kc + 1) * TT],
                                                           start=(kc == 0), stop=(kc == KCH - 1))
                                return ins
                            return go
                        em("tensor", gq + [(ve, x2ln_v)] + gw(("pb", gbank), ("pb", ubank)),
                           gumm(nb, sub, gbank, ubank), pe, 1)
                        gu_p = pe.n
                        if sub == 1:
                            gs(("fg", nb), pe)
                            gs(("fu", nb), pe)
                        em("scalar", [(pe, gu_p)] + gw(("fgg", sub)), (lambda gbank, sub: lambda e: e.activation(
                            f_gg[sub][:], psum[gbank][:, 0:TT], AF.Gelu_apprx_tanh))(gbank, sub), scm, 1)
                        gs(("pb", gbank), scm)
                        em("vector", [(scm, scm.n)], (lambda lnch, ubank, sub: lambda e: e.tensor_mul(
                            f_int[:, lnch * TT:(lnch + 1) * TT], f_gg[sub][:],
                            psum[ubank][:, 0:TT]))(lnch, ubank, sub), ve, 1)
                        gs(("pb", ubank), ve)
                        gs(("fgg", sub), ve)
            int_v = ve.n
            gs("x2ln", pe)
            if debug and t == 0 and ih == 0:
                em("gpsimd", [(ve, int_v)], lambda e: e.dma_start(
                    out=fdbg1.ap()[:, 0:64 * TT], in_=f_int[:, 0:64 * TT]), sems["qe"], 16)
                em("gpsimd", (), lambda e: e.dma_start(
                    out=fdbg3.ap()[:, 0:KCH * TT], in_=o_x2ln[:]), sems["qe"], 16)
                dbg_gate = [(sems["qe"], sems["qe"].n)]

            if debug and t == 0 and ih == 1:
                em("gpsimd", [(ve, ve.n)], lambda e: e.dma_start(
                    out=fdbg2.ap()[:, 0:KCH * TT], in_=f_acc[:]), sems["qe"], 16)
                guard["oy2"] = guard.get("oy2", []) + [(sems["qe"], sems["qe"].n)]
            for d2c in range(8):
                if ih == 1:
                    for sub in range(2):
                        dch = d2c * 2 + sub
                        em("gpsimd", gw(("x2p", sub), "obc") + [(qx2, 16 * (t + 1))], (lambda dch, t, sub: lambda e: e.dma_start(
                            out=f_x2p[sub][:],
                            in_=rap(x2_sp, dch * 128 * T + t * TT, (T, 128), (1, TT))))(dch, t, sub), qx2p[sub], 16)
                    x2p_w = [(qx2p[0], qx2p[0].n), (qx2p[1], qx2p[1].n)]
                for r in range(4 * ih, 4 * ih + 4):
                    db = r % 2
                    em("sync", gw(("fd", db)), (lambda d2c, r, db: lambda e: e.dma_start(
                        out=f_d[db][:].rearrange("p (a c) -> p a c", a=16),
                        in_=rap(wgth, r * 6144 * 2048 + 2 * 2048 * 2048 + d2c * 2048 * 256,
                                (256, 128), (128 * 256, 16), (1, 256))))(d2c, r, db), qfd[db], 16)
                    dq_w = (qfd[db], qfd[db].n)

                    def dmm(r, db, ih):
                        def go(e):
                            for nct in range(16):
                                lnch = (r - 4 * ih) * 16 + nct
                                for sub in range(2):
                                    ins = nc.tensor.matmul(
                                        psum[sub][:, 0:TT],
                                        f_d[db][:, nct * 256 + sub * 128:nct * 256 + sub * 128 + 128],
                                        f_int[:, lnch * TT:(lnch + 1) * TT],
                                        start=(r == 4 * ih and nct == 0),
                                        stop=(r == 4 * ih + 3 and nct == 15))
                            return ins
                        return go
                    dw = [dq_w, (ve, int_v)]
                    if r == 4 * ih:
                        dw += gw(("pbd", 0), ("pbd", 1))
                    em("tensor", dw, dmm(r, db, ih), pe, 1)
                    gs(("fd", db), pe)
                d_p = pe.n
                if False and debug and t == 1 and ih == 1 and d2c == 0:
                    em("vector", [(pe, d_p)], lambda e: e.tensor_copy(o_bc1[:].bitcast(F32)[:, 0:256], psum[0][:, 0:256]), ve, 1)
                    em("gpsimd", [(ve, ve.n)], lambda e: e.dma_start(out=fdbg4.ap()[:, 0:256], in_=o_bc1[:].bitcast(F32)[:, 0:256]), sems["qe"], 16)
                    em("gpsimd", (), lambda e: e.dma_start(out=fdbg5.ap()[:, 0:TT], in_=f_acc[:, 0:TT]), sems["qe"], 16)
                    em("gpsimd", [(qx2p[0], qx2p[0].n)], lambda e: e.dma_start(out=fdbg6.ap()[:, 0:TT], in_=f_x2p[0][:]), sems["qe"], 16)
                    em("gpsimd", (), lambda e: e.dma_start(out=fdbg7.ap()[:, 0:64 * TT], in_=f_int[:, 0:64 * TT]), sems["qe"], 16)
                for sub in range(2):
                    dch = d2c * 2 + sub
                    if ih == 0:
                        em("vector", [(pe, d_p)], (lambda dch, sub: lambda e: e.tensor_copy(
                            f_acc[:, dch * TT:(dch + 1) * TT], psum[sub][:, 0:TT]))(dch, sub), ve, 1)
                        gs(("pbd", sub), ve)
                    else:
                        em("vector", [(pe, d_p)] + x2p_w + [(qo[sub], qo[sub].n)],
                           (lambda dch, sub: lambda e: e.tensor_add(
                               f_o[sub][:], f_acc[:, dch * TT:(dch + 1) * TT],
                               psum[sub][:, 0:TT]))(dch, sub), ve, 1)
                        em("vector", (), (lambda sub: lambda e: e.tensor_add(
                            f_o[sub][:], f_o[sub][:], f_x2p[sub][:]))(sub), ve, 1)
                        gs(("pbd", sub), ve)
                        em("gpsimd", [(ve, ve.n)], (lambda dch, t, sub: lambda e: e.dma_start(
                            out=rap(out, dch * 128 * T + t * TT, (T, 128), (1, TT)),
                            in_=f_o[sub][:]))(dch, t, sub), qo[sub], 16)
                        gs(("x2p", sub), ve)

    return finish()


def _perm_k():
    return np.concatenate([np.arange(1024) * 2, np.arange(1024) * 2 + 1])


def _perm_n():
    p = np.empty(NMLP, np.int64)
    j = np.arange(1024)
    for cr in range(8):
        p[2048 * cr + j] = 2 * (1024 * cr + j)
        p[2048 * cr + 1024 + j] = 2 * (1024 * cr + j) + 1
    return p


def _pack_u8(p32):
    return np.ascontiguousarray(np.asarray(p32).astype(np.int32).view(np.uint8)[:, 0::4])


def _host_prep(inp, T):
    TM = T // 4
    pk = _perm_k()
    pn = _perm_n()

    xr = np.asarray(inp["x"], np.float32).reshape(-1, DM)
    x = np.ascontiguousarray(xr.T[pk])
    w1t_h = np.ascontiguousarray(np.asarray(inp["w1"], np.float32).T[pk])
    xp = np.asarray(inp["wx"], np.float32)
    wxt_h = np.zeros((DI, 176), np.float32)
    wxt_h[:, :128] = xp[:128].T
    wxt_h[:, 128:144] = xp[128:144].T
    wxt_h[:, 160:176] = xp[144:160].T
    wdt_h = np.ascontiguousarray(np.asarray(inp["wd"], np.float32).T)
    wot_h = np.ascontiguousarray(np.asarray(inp["wo"], np.float32).T[:, pk])

    def chl(v, n):
        return np.ascontiguousarray(np.asarray(v, np.float32).reshape(n, 128).T)

    convw_h = np.ascontiguousarray(np.asarray(inp["cw"], np.float32)
                                   .reshape(ICH, 128, DC).transpose(1, 0, 2).reshape(128, ICH * DC))
    dtb_h = chl(inp["db"], ICH)
    dvec_h = chl(inp["dv"], ICH)
    alog_h = np.ascontiguousarray(np.asarray(inp["al"], np.float32)
                                  .reshape(ICH, 128, DN).transpose(1, 0, 2).reshape(128, ICH * DN))

    g8T = np.ascontiguousarray(_pack_u8(inp["gp"])[pn].T)
    u8T = np.ascontiguousarray(_pack_u8(inp["up"])[pn].T)
    d8T = np.ascontiguousarray(_pack_u8(inp["dp"])[pk].T)
    gsT = np.ascontiguousarray(np.asarray(inp["gs"], np.float32)[pn].T)
    usT = np.ascontiguousarray(np.asarray(inp["us"], np.float32)[pn].T)
    dsT = np.ascontiguousarray(np.asarray(inp["ds"], np.float32)[pk].T)

    idm_h = np.eye(128, dtype=np.float32)
    ones_h = np.ones((128, 1), np.float32)
    kc_h = np.zeros((128, 8), np.float32)
    kc_h[:, 0] = 1e-5
    kc_h[:, 1] = -0.5
    kc_h[:, 2] = -1.0
    kc_h[:, 3] = 1.0
    kc_h[:, 4] = -4.0
    kc_h[:, 5] = -6.0

    ncores = NCORE
    tpc = x.shape[1] // ncores
    assert tpc >= T, (tpc, T)

    in_maps = []
    for cr in range(ncores):
        t0 = cr * tpc
        xw = np.zeros((DM, TM + T), np.float32)
        if cr % 4 != 0:
            xw[:, :TM] = x[:, t0 - TM:t0]
        xw[:, TM:] = x[:, t0:t0 + T]
        mdv = np.zeros((128, 8), np.float32)
        mev = np.zeros((128, 8), np.float32)
        for r in range(4 * (cr // 4), cr):
            mdv[:, r] = 1.0
            mev[:, r] = 1.0
        in_maps.append({
            "xin": np.ascontiguousarray(xw),
            "w1t": w1t_h, "wxt": wxt_h, "wdt": wdt_h, "wot": wot_h,
            "convw": convw_h, "dtb": dtb_h, "dvec": dvec_h, "alog": alog_h,
            "maskd": mdv, "maskdc": np.ascontiguousarray(1.0 - mdv), "maske": mev,
            "idm": idm_h, "onescol": ones_h, "kcst": kc_h,
            "gpk": np.ascontiguousarray(g8T[:, cr * 2048:(cr + 1) * 2048]),
            "upk": np.ascontiguousarray(u8T[:, cr * 2048:(cr + 1) * 2048]),
            "dpk": np.ascontiguousarray(d8T[cr * 1024:(cr + 1) * 1024]),
            "gsc": np.ascontiguousarray(gsT[:, cr * 2048:(cr + 1) * 2048]),
            "usc": np.ascontiguousarray(usT[:, cr * 2048:(cr + 1) * 2048]),
            "dsc": np.ascontiguousarray(dsT[64 * cr:64 * (cr + 1)]),
        })
    return in_maps


def kernel(L0_attn_hidden_states, L0_attn_in_proj_weight, L0_attn_conv1d_weight,
           L0_attn_x_proj_weight, L0_attn_dt_proj_weight, L0_attn_dt_proj_bias,
           L0_attn_A_log, L0_attn_D, L0_attn_out_proj_weight,
           L0_ffn_gate_W_packed, L0_ffn_gate_scales, L0_ffn_up_W_packed,
           L0_ffn_up_scales, L0_ffn_down_W_packed, L0_ffn_down_scales):
    T = 2048
    inp = {"x": L0_attn_hidden_states, "w1": L0_attn_in_proj_weight,
           "cw": L0_attn_conv1d_weight, "wx": L0_attn_x_proj_weight,
           "wd": L0_attn_dt_proj_weight, "db": L0_attn_dt_proj_bias,
           "al": L0_attn_A_log, "dv": L0_attn_D, "wo": L0_attn_out_proj_weight,
           "gp": L0_ffn_gate_W_packed, "gs": L0_ffn_gate_scales,
           "up": L0_ffn_up_W_packed, "us": L0_ffn_up_scales,
           "dp": L0_ffn_down_W_packed, "ds": L0_ffn_down_scales}
    in_maps = _host_prep(inp, T)
    nc = build_nc(T=T)
    res = run_bass_kernel_spmd(nc, in_maps, core_ids=list(range(NCORE)))
    outs = [np.asarray(res.results[cr]["out"]) for cr in range(NCORE)]
    full = np.concatenate(outs, axis=1)
    inv_pk = np.argsort(_perm_k())
    return np.ascontiguousarray(full[inv_pk].T).reshape(2, 8192, DM).astype(np.float32)
